# version 32
# baseline (speedup 1.0000x reference)
import numpy as np
import ml_dtypes
from contextlib import ExitStack

import concourse.bass as bass
import concourse.mybir as mybir
from concourse.bass_utils import run_bass_kernel_spmd

F32 = mybir.dt.float32
BF16 = mybir.dt.bfloat16
I32 = mybir.dt.int32
AL = mybir.AluOpType
AF = mybir.ActivationFunctionType
AX = mybir.AxisListType

D = 1024
H = 16
DK = 64
L = 4096
HID = 3072
EPS = 1e-5
CH = 128
TA = 512
TB = 256
NTA = L // TA
NTB = L // TB
NSA = TA // 128
NSB = TB // 128
NCH = TA // CH
MAGIC = 0x5F3759DF
SC_FF = 1.0 / 127.0
SC_DN = 1.0 / (127.0 * 127.0)

ENGINES = ("sync", "tensor", "vector", "scalar", "gpsimd")


class Sem:
    __slots__ = ("name", "n", "h")

    def __init__(self, name):
        self.name = name
        self.n = 0
        self.h = None


class B:

    FAST = True

    def __init__(self):
        self.ops = {e: [] for e in ENGINES}
        self.sems = []
        self.auto = {}

    def sem(self, name):
        s = Sem(name)
        self.sems.append(s)
        return s

    def op(self, eng, fn, waits=(), inc=None, amt=1):
        w = [(s, v) for (s, v) in waits if v > 0]
        a = self.auto.get(eng)
        if a is not None and a.n > 0 and not B.FAST:
            w.append((a, a.n))
        self.ops[eng].append((w, fn, inc, amt))
        if inc is not None:
            inc.n += amt
            return inc.n
        return None


def emit(nc, b, ctx):
    for s in b.sems:
        s.h = ctx.enter_context(nc.semaphore(s.name))
    final = {id(s): s.n for s in b.sems}
    for e in ENGINES:
        for w, _, _, _ in b.ops[e]:
            for s, v in w:
                assert v <= final[id(s)], \
                    f"wait {s.name}>={v} unreachable (final {final[id(s)]})"
    block = ctx.enter_context(nc.Block())

    def run(eng, ops):
        seen = {}
        for w, fn, inc, amt in ops:
            for s, v in w:
                if seen.get(id(s), -1) < v:
                    eng.wait_ge(s.h, v)
                    seen[id(s)] = v
            ins = fn(eng)
            if inc is not None:
                assert ins is not None
                ins.then_inc(inc.h, amt)

    block.sync(lambda e: run(e, b.ops["sync"]))
    block.tensor(lambda e: run(e, b.ops["tensor"]))
    block.vector(lambda e: run(e, b.ops["vector"]))
    block.scalar(lambda e: run(e, b.ops["scalar"]))
    block.gpsimd(lambda e: run(e, b.ops["gpsimd"]))


def build(n_tiles_a=NTA, n_tiles_b=NTB):
    nc = bass.Bass("TRN2", target_bir_lowering=False)

    x_in = nc.dram_tensor("x_in", [L, D], F32, kind="ExternalInput").ap()
    wqkv = nc.dram_tensor("wqkv", [D, 3 * D], F32, kind="ExternalInput").ap()
    wo = nc.dram_tensor("wo", [D, D], F32, kind="ExternalInput").ap()
    wgt = nc.dram_tensor("wgt", [D, HID], I32, kind="ExternalInput").ap()
    wut = nc.dram_tensor("wut", [D, HID], I32, kind="ExternalInput").ap()
    wdt = nc.dram_tensor("wdt", [HID, D], I32, kind="ExternalInput").ap()
    maskd = nc.dram_tensor("maskd", [128, 128], BF16, kind="ExternalInput").ap()
    identd = nc.dram_tensor("identd", [128, 128], BF16, kind="ExternalInput").ap()
    out_d = nc.dram_tensor("out", [L, D], F32, kind="ExternalOutput").ap()
    x2scr = nc.dram_tensor("x2scr", [L, D], F32).ap()

    b = B()
    d_x0 = b.sem("d_x0")
    d_x = b.sem("d_x")
    d_wa = b.sem("d_wa")
    d_wb = b.sem("d_wb")
    d_ms = b.sem("d_ms")
    d_x2w = b.sem("d_x2w")
    d_x2r = b.sem("d_x2r")
    d_out = b.sem("d_out")
    sp = b.sem("sp")
    sv = b.sem("sv")
    ss = b.sem("ss")
    sg = b.sem("sg")
    b.auto = {"vector": sv, "scalar": ss, "gpsimd": sg}

    ctx = ExitStack()

    def sb(name, shape, dt):
        return ctx.enter_context(nc.sbuf_tensor(name, shape, dt))

    mask_sb = sb("mask_sb", [128, 128], BF16)
    ident_sb = sb("ident_sb", [128, 128], BF16)
    magic_sb = sb("magic_sb", [128, 64], I32)
    sumx = sb("sumx", [128, 96], F32)
    sumq = sb("sumq", [128, 96], F32)
    aa = sb("aa", [128, 96], F32)
    bb = sb("bb", [128, 96], F32)
    c1 = sb("c1", [128, 64], F32)
    c2 = sb("c2", [128, 64], F32)
    c4 = sb("c4", [128, 64], F32)
    ish = sb("ish", [128, 64], I32)
    trash_v = sb("trash_v", [128, 1024], F32)
    sq_sb = sb("sq_sb", [128, 1024], F32)

    aa_guard = [(ss, 0)]
    ln_v = [0]

    def ln_chain(sl, N, last_v, guard=None):
        def w(v):
            return [(sv, v)]
        z = slice(0, sl.stop - sl.start)
        v = last_v
        v = b.op("vector", lambda e, s=sl, z=z: e.tensor_scalar_mul(c1[:, z], sumx[:, s], 1.0 / N),
                 w(v) + [guard if guard is not None else aa_guard[0]], sv)
        v = b.op("vector", lambda e, z=z: e.tensor_mul(c4[:, z], c1[:, z], c1[:, z]), w(v), sv)
        v = b.op("vector", lambda e, s=sl, z=z: e.scalar_tensor_tensor(
            c2[:, z], sumq[:, s], 1.0 / N, c4[:, z], op0=AL.mult, op1=AL.subtract), w(v), sv)
        v = b.op("vector", lambda e, z=z: e.tensor_scalar_add(c2[:, z], c2[:, z], EPS), w(v), sv)
        v = b.op("vector", lambda e, z=z: e.tensor_scalar(
            ish[:, z], c2[:, z].bitcast(I32), 1, None, op0=AL.arith_shift_right), w(v), sv)
        v = b.op("vector", lambda e, s=sl, z=z: e.tensor_tensor(
            aa[:, s].bitcast(I32), magic_sb[:, z], ish[:, z], op=AL.subtract), w(v), sv)
        for _ in range(3):
            v = b.op("vector", lambda e, s=sl, z=z: e.tensor_mul(c4[:, z], aa[:, s], aa[:, s]), w(v), sv)
            v = b.op("vector", lambda e, z=z: e.scalar_tensor_tensor(
                c4[:, z], c4[:, z], -0.5, c2[:, z], op0=AL.mult, op1=AL.mult), w(v), sv)
            v = b.op("vector", lambda e, s=sl, z=z: e.scalar_tensor_tensor(
                aa[:, s], c4[:, z], 1.5, aa[:, s], op0=AL.add, op1=AL.mult), w(v), sv)
        v = b.op("vector", lambda e, s=sl, z=z: e.scalar_tensor_tensor(
            bb[:, s], c1[:, z], -1.0, aa[:, s], op0=AL.mult, op1=AL.mult), w(v), sv)
        return v

    def ln_chain_part1(sl, N, last_v, guard=None, swait=None):
        def w(v):
            return [(sv, v)]
        z = slice(0, sl.stop - sl.start)
        w0 = w(last_v) + ([guard] if guard is not None else [])
        if swait is not None:
            w0.append(swait)
        v = b.op("vector", lambda e, s=sl, z=z: e.tensor_scalar_mul(c1[:, z], sumx[:, s], 1.0 / N),
                 w0, sv)
        v = b.op("vector", lambda e, z=z: e.tensor_mul(c4[:, z], c1[:, z], c1[:, z]), w(v), sv)
        v = b.op("vector", lambda e, s=sl, z=z: e.scalar_tensor_tensor(
            c2[:, z], sumq[:, s], 1.0 / N, c4[:, z], op0=AL.mult, op1=AL.subtract), w(v), sv)
        v = b.op("vector", lambda e, z=z: e.tensor_scalar_add(c2[:, z], c2[:, z], EPS), w(v), sv)
        v = b.op("vector", lambda e, z=z: e.tensor_scalar(
            ish[:, z], c2[:, z].bitcast(I32), 1, None, op0=AL.arith_shift_right), w(v), sv)
        v = b.op("vector", lambda e, s=sl, z=z: e.tensor_tensor(
            aa[:, s].bitcast(I32), magic_sb[:, z], ish[:, z], op=AL.subtract), w(v), sv)
        return v

    def ln_chain_part2(sl, last_v):
        def w(v):
            return [(sv, v)]
        z = slice(0, sl.stop - sl.start)
        v = last_v
        for _ in range(3):
            v = b.op("vector", lambda e, s=sl, z=z: e.tensor_mul(c4[:, z], aa[:, s], aa[:, s]), w(v), sv)
            v = b.op("vector", lambda e, z=z: e.scalar_tensor_tensor(
                c4[:, z], c4[:, z], -0.5, c2[:, z], op0=AL.mult, op1=AL.mult), w(v), sv)
            v = b.op("vector", lambda e, s=sl, z=z: e.scalar_tensor_tensor(
                aa[:, s], c4[:, z], 1.5, aa[:, s], op0=AL.add, op1=AL.mult), w(v), sv)
        v = b.op("vector", lambda e, s=sl, z=z: e.scalar_tensor_tensor(
            bb[:, s], c1[:, z], -1.0, aa[:, s], op0=AL.mult, op1=AL.mult), w(v), sv)
        return v

    actxA = ExitStack()

    def sba(name, shape, dt):
        return actxA.enter_context(nc.sbuf_tensor(name, shape, dt))

    wq_sb = sba("wq_sb", [128, 8, 1024], BF16)
    wkv_sb = sba("wkv_sb", [128, 8, 2048], BF16)
    h1_sb = sba("h1_sb", [128, NSA, 1024], BF16)
    h1t_sb = sba("h1t_sb", [128, 8, TA], BF16)
    ksq_sb = sba("ksq_sb", [128, 1024], F32)
    ksq2_sb = sba("ksq2_sb", [128, 1024], F32)
    q_sb = sba("q_sb", [128, 8, TA], BF16)
    kn_sb = sba("kn_sb", [128, NSA, 1024], BF16)
    kt_sb = sba("kt_sb", [128, 8, TA], BF16)
    v_sb = sba("v_sb", [128, NSA, 1024], BF16)
    sc_sb = sba("sc_sb", [128, 16, 256], BF16)
    scr_sb = sba("scr_sb", [128, 2, 512], BF16)
    st_sb = sba("st_sb", [128, 8, 64], BF16)
    stateF = sba("stateF", [128, 512], F32)
    y_sb = sba("y_sb", [128, 8, TA], BF16)
    wo_sb = sba("wo_sb", [128, 8, 1024], BF16)
    k_sb = sba("k_sb", [128, NSA, 1024], F32)
    x_sb = sba("x_sb", [128, 2, NSA, 1024], F32)

    pctxA = ExitStack()

    def psa(name, shape):
        return pctxA.enter_context(nc.psum_tensor(name, shape, F32))

    delta_ps = psa("delta_ps", [128, 512])
    y_psA = psa("y_psA", [128, 512])
    y_psB = psa("y_psB", [128, 512])
    scE_ps = psa("scE_ps", [128, 512])
    scO_ps = psa("scO_ps", [128, 512])
    rot = [psa(f"rot{i}", [128, 512]) for i in range(3)]

    b.op("vector", lambda e: e.memset(magic_sb[:, :], MAGIC), (), sv)
    b.op("vector", lambda e: e.memset(stateF[:, :], 0.0), (), sv)
    for s0_ in range(NSA):
        b.op("sync", lambda e, s=s0_: e.dma_start(
            x_sb[:, 0, s:s + 1], x_in[s * 128:(s + 1) * 128, :].rearrange(
                "(s p) c -> p s c", p=128)), (), d_x0, 16)
    b.op("sync", lambda e: e.dma_start(mask_sb[:, :], maskd), (), d_ms, 16)
    b.op("sync", lambda e: e.dma_start(ident_sb[:, :], identd), [(d_ms, 16)], d_ms, 16)
    b.op("scalar", lambda e: e.preload_act_table(AF.Silu), (), None)
    b.op("gpsimd", lambda e: e.dma_start(
        wq_sb[:, :, :], wqkv[:, 0:1024].rearrange("(j p) c -> p j c", p=128)), (), d_wa, 16)
    b.op("gpsimd", lambda e: e.dma_start(
        wkv_sb[:, :, :], wqkv[:, 1024:3072].rearrange("(j p) c -> p j c", p=128)),
        [(d_wa, 16)], d_wa, 16)
    b.op("gpsimd", lambda e: e.dma_start(
        wo_sb[:, :, :], wo.rearrange("(j p) c -> p j c", p=128)), [(d_wa, 32)], d_wa, 16)

    rot_state = [{"free": (sv, 0)} for _ in range(3)]
    rot_i = [0]

    def rot_next():
        i = rot_i[0] % 3
        rot_i[0] += 1
        return i

    gate_clsA_sp = [0]
    gate_clsA_sv = [0]
    gate_clsA_ss = [0]
    wdma_n = [0]
    wg_rdy = {}
    wu_rdy = {}
    wtens = {}

    def emit_wg_dmas():
        for ci in range(4):
            w0 = ([(sp, gate_clsA_sp[0]), (sv, gate_clsA_sv[0]),
                   (ss, gate_clsA_ss[0]), (d_wa, 48)]
                  if wdma_n[0] == 0 else [(d_wb, 16 * wdma_n[0])])
            b.op("gpsimd", lambda e, ci=ci: e.dma_start(
                wtens["wg"][:, :, ci * 768:(ci + 1) * 768],
                wgt[:, ci * 768:(ci + 1) * 768].rearrange("(j p) c -> p j c", p=128)),
                w0, d_wb, 16)
            wg_rdy[ci] = 16 * (wdma_n[0] + 1)
            wdma_n[0] += 1

    def emit_wu_dmas():
        for ci in range(2):
            w0 = [(sp, gate_clsC_sp[0]), (ss, gate_clsC_ss[0]),
                  (sv, gate_clsC_sv[0]), last_stcopy[0],
                  (d_wb, 16 * wdma_n[0])]
            b.op("gpsimd", lambda e, ci=ci: e.dma_start(
                wtens["wu"][:, :, ci * 1536:(ci + 1) * 1536],
                wut[:, ci * 1536:(ci + 1) * 1536].rearrange("(j p) c -> p j c", p=128)),
                w0, d_wb, 16)
            wu_rdy[ci] = 16 * (wdma_n[0] + 1)
            wdma_n[0] += 1
    gate_clsB_sp = [0]
    gate_clsB_sv = [0]
    gate_clsB_ss = [0]
    gate_clsC_sp = [0]
    gate_clsC_ss = [0]
    gate_clsC_sv = [0]
    last_upd = [0]
    gscore = [0]
    upd_m_g = {}
    trash_tok = [0]
    sq_tok = [0]
    last_sc_evac = [0, 0]
    last_mask_v = [0]
    last_y_evac = [0]
    y_ev_c = [0, 0, 0, 0]
    last_inter = [0]
    last_stcopy = [(sg, 0)]
    last_stcopy_v = [0]
    h1_read_m = {}
    attn_free = [(sv, 0)]
    stat_guard = {}
    sum_guard = {}
    gchunk = [0]
    blocks = [(s, j) for s in range(NSA) for j in range(8)]

    def a_ln_pieces(tt):
        par = tt % 2
        base = 64 + 4 * par
        st_ = {}

        def xw(s):
            return (d_x0, 16 * (s + 1)) if tt == 0 else (d_x, 16 * tt)

        def p0a():
            lastv = 0
            for s in (0, 1):
                lastv = b.op("vector", lambda e, s=s, par=par, base=base: e.tensor_scalar(
                    trash_v[:, :], x_sb[:, par, s], 1.0, 0.0, op0=AL.mult, op1=AL.add,
                    accum_out=sumx[:, base + s:base + s + 1]),
                    [xw(s), sum_guard.get(base, (sv, 0)), (sv, trash_tok[0])], sv)
                trash_tok[0] = lastv

        def p0b():
            lastv = 0
            for s in (2, 3):
                lastv = b.op("vector", lambda e, s=s, par=par, base=base: e.tensor_scalar(
                    trash_v[:, :], x_sb[:, par, s], 1.0, 0.0, op0=AL.mult, op1=AL.add,
                    accum_out=sumx[:, base + s:base + s + 1]),
                    [xw(s), sum_guard.get(base, (sv, 0)), (sv, trash_tok[0])], sv)
                trash_tok[0] = lastv
            lasts = 0
            for s in range(NSA):
                lasts = b.op("scalar", lambda e, s=s, par=par, base=base: e.activation(
                    sq_sb[:, :], x_sb[:, par, s], AF.Square,
                    accum_out=sumq[:, base + s:base + s + 1]),
                    [xw(s), sum_guard.get(base, (sv, 0)), (ss, sq_tok[0])], ss)
                sq_tok[0] = lasts
            st_["v"] = lastv
            st_["s"] = lasts

        def p1():
            st_["c1"] = ln_chain_part1(slice(base, base + NSA), D, st_["v"],
                                       guard=stat_guard.get(base), swait=(ss, st_["s"]))

        def p2a():
            vch = ln_chain_part2(slice(base, base + NSA), st_["c1"])
            sum_guard[base] = (sv, vch)
            st_["vch"] = vch

        def p2b(s):
            h1_last = b.op("scalar", lambda e, s=s, par=par, base=base: e.activation(
                h1_sb[:, s], x_sb[:, par, s], AF.Identity,
                bias=bb[:, base + s:base + s + 1], scale=aa[:, base + s:base + s + 1]),
                [(sv, st_["vch"]), (sp, h1_read_m.get(0, 0))], ss)
            if s == NSA - 1:
                stat_guard[base] = (ss, h1_last)
            return h1_last
        return p0a, p0b, p1, p2a, p2b

    def a_ln(tt):
        p0a, p0b, p1, p2a, p2b = a_ln_pieces(tt)
        p0a(); p0b(); p1(); p2a()
        h1_last = 0
        h1_s = []
        for s in range(NSA):
            h1_last = p2b(s)
            h1_s.append(h1_last)
        return h1_last, h1_s

    trans_m = {}
    resid_done = [0]
    h2_pend0 = [0]

    def b_ln_pieces(tt):
        parB = tt % 2
        baseB = 80 + 4 * parB
        stB = {}

        def p0():
            if tt >= 2:
                waits = [(d_x2w, 16 * (tt * TB // TA + 1)), (d_x2r, 16 * tt),
                         (sg, resid_done[0])]
                b.op("sync", lambda e: e.dma_start(
                    wtens["x2"][:, parB], x2scr[tt * TB:(tt + 1) * TB, :].rearrange(
                        "(s p) c -> p s c", p=128)), waits, d_x2r, 16)
            x2r = 16 * (tt + 1)
            lastv = 0
            for s in range(NSB):
                lastv = b.op("vector", lambda e, s=s: e.tensor_scalar(
                    trash_v[:, :], wtens["x2"][:, parB, s], 1.0, 0.0, op0=AL.mult, op1=AL.add,
                    accum_out=sumx[:, baseB + s:baseB + s + 1]),
                    [(d_x2r, x2r), (sv, trash_tok[0])], sv)
                trash_tok[0] = lastv
            lasts = 0
            sg0 = [sum_guard.get(baseB, (sv, 0))]
            for s in range(NSB):
                lasts = b.op("scalar", lambda e, s=s: e.activation(
                    sq_sb[:, :], wtens["x2"][:, parB, s], AF.Square,
                    accum_out=sumq[:, baseB + s:baseB + s + 1]),
                    [(d_x2r, x2r), (ss, sq_tok[0])] + (sg0 if s == 0 else []), ss)
                sq_tok[0] = lasts
            stB["lastv"], stB["lasts"] = lastv, lasts

        def p1():
            b.op("vector", lambda e: None, [(ss, stB["lasts"])])
            vch = ln_chain(slice(baseB, baseB + NSB), D, stB["lastv"])
            sum_guard[baseB] = (sv, vch)
            stB["vch"] = vch

        def p2():
            h2_last = 0
            for s in range(NSB):
                h2_last = b.op("vector", lambda e, s=s: e.tensor_scalar(
                    wtens["h2"][:, parB, s], wtens["x2"][:, parB, s],
                    aa[:, baseB + s:baseB + s + 1],
                    bb[:, baseB + s:baseB + s + 1], op0=AL.mult, op1=AL.add),
                    [(sv, stB["vch"]), (sp, trans_m.get(parB, 0))] +
                    ([(sp, gate_clsB_sp[0])] if tt == 0 else []), sv)
            return h2_last
        return [p0, p1, p2]

    h1_pend, h1_pend_s = (a_ln(0) if n_tiles_a > 0 else (0, [0] * NSA))

    for t in range(n_tiles_a):
        par = t % 2
        if t + 1 < n_tiles_a:
            tt = t + 1
            waits = [(d_x, 16 * (tt - 1))]
            if tt >= 2:
                waits.append((d_x2w, 16 * (tt - 1)))
            b.op("sync", lambda e, tt=tt: e.dma_start(
                x_sb[:, tt % 2], x_in[tt * TA:(tt + 1) * TA, :].rearrange("(s p) c -> p s c", p=128)),
                waits, d_x, 16)

        h1t_ready = 0
        for g in range(0, len(blocks), 4):
            grp = blocks[g:g + 4]
            s0, j0 = grp[0]
            ri = rot_next()
            hw_ = h1_pend_s[s0] if t == 0 else h1_pend
            waits = [(ss, hw_), (d_ms, 32), rot_state[ri]["free"]]
            for gi, (s, j) in enumerate(grp):
                m = b.op("tensor", lambda e, ri=ri, gi=gi, s=s, j=j, par=par: e.matmul(
                    rot[ri][:, gi * 128:(gi + 1) * 128], h1_sb[:, s, j * 128:(j + 1) * 128],
                    ident_sb[:, :], start=True, stop=True),
                    waits if gi == 0 else (), sp if gi == 3 else None)
            h1t_ready = b.op("scalar", lambda e, ri=ri, s0=s0, j0=j0: e.copy(
                h1t_sb[:, j0:j0 + 4, s0 * 128:(s0 + 1) * 128],
                rot[ri][:, :].rearrange("p (g c) -> p g c", c=128)), [(sp, m)], ss)
            rot_state[ri]["free"] = (ss, h1t_ready)
        h1_read_m[0] = sp.n

        q_done = 0
        for mi in range(8):
            ri = rot_next()
            waits = [(ss, h1t_ready), (d_wa, 16), rot_state[ri]["free"]]
            for k in range(8):
                m = b.op("tensor", lambda e, ri=ri, mi=mi, k=k: e.matmul(
                    rot[ri][:, 0:512], wq_sb[:, k, mi * 128:(mi + 1) * 128],
                    h1t_sb[:, k, :], start=(k == 0), stop=(k == 7)),
                    waits if k == 0 else (), sp if k == 7 else None)
            q_done = b.op("scalar", lambda e, ri=ri, mi=mi: e.copy(
                q_sb[:, mi, :], rot[ri][:, 0:512]), [(sp, m)], ss)
            rot_state[ri]["free"] = (ss, q_done)

        kv_k = 0
        kv_v = 0
        first_kev = True
        for nn in (0, 1):
            for s in range(NSA):
                ri = rot_next()
                waits = [(ss, h1t_ready), (d_wa, 32), rot_state[ri]["free"]]
                for k in range(8):
                    m = b.op("tensor", lambda e, ri=ri, s=s, nn=nn, k=k: e.matmul(
                        rot[ri][:, 0:512], h1t_sb[:, k, s * 128:(s + 1) * 128],
                        wkv_sb[:, k, nn * 512:(nn + 1) * 512], start=(k == 0), stop=(k == 7)),
                        waits if k == 0 else (), sp if k == 7 else None)
                ew = [(sp, m)]
                if first_kev:
                    ew.append(attn_free[0])
                    first_kev = False
                kv_k = b.op("scalar", lambda e, ri=ri, s=s, nn=nn: e.copy(
                    k_sb[:, s, nn * 512:(nn + 1) * 512], rot[ri][:, 0:512]), ew, ss)
                rot_state[ri]["free"] = (ss, kv_k)
        lastv = 0
        for s in range(NSA):
            lastv = b.op("vector", lambda e, s=s: e.reduce_sum(
                sumx[:, s * 16:(s + 1) * 16],
                k_sb[:, s].rearrange("p (h e) -> p h e", e=64), axis=AX.X),
                [(ss, kv_k)] + ([sum_guard.get(0, (sv, 0))] if s == 0 else []), sv)
        prevq = lastv
        redq = [0, 0]
        for s in range(NSA):
            buf_k = ksq_sb if s % 2 == 0 else ksq2_sb
            sx = b.op("scalar", lambda e, s=s, buf_k=buf_k: e.activation(
                buf_k[:, :], k_sb[:, s], AF.Square),
                [(ss, kv_k), (sv, redq[s % 2])], ss)
            prevq = b.op("vector", lambda e, s=s, buf_k=buf_k: e.reduce_sum(
                sumq[:, s * 16:(s + 1) * 16],
                buf_k[:, :].rearrange("p (h e) -> p h e", e=64), axis=AX.X),
                [(ss, sx)], sv)
            redq[s % 2] = prevq
        for nn in (2, 3):
            for s in range(NSA):
                ri = rot_next()
                waits = [(ss, h1t_ready), (d_wa, 32), rot_state[ri]["free"]]
                for k in range(8):
                    m = b.op("tensor", lambda e, ri=ri, s=s, nn=nn, k=k: e.matmul(
                        rot[ri][:, 0:512], h1t_sb[:, k, s * 128:(s + 1) * 128],
                        wkv_sb[:, k, nn * 512:(nn + 1) * 512], start=(k == 0), stop=(k == 7)),
                        waits if k == 0 else (), sp if k == 7 else None)
                kv_v = b.op("scalar", lambda e, ri=ri, s=s, nn=nn: e.copy(
                    v_sb[:, s, (nn - 2) * 512:(nn - 1) * 512], rot[ri][:, 0:512]), [(sp, m)], ss)
                rot_state[ri]["free"] = (ss, kv_v)
        if t == n_tiles_a - 1:
            gate_clsA_sp[0] = sp.n
            gate_clsA_ss[0] = ss.n
            gate_clsA_sv[0] = sv.n
            emit_wg_dmas()
            for bt_ in range(2):
                b.op("sync", lambda e, bt=bt_: e.dma_start(
                    wtens["x2"][:, bt], x2scr[bt * TB:(bt + 1) * TB, :].rearrange(
                        "(s p) c -> p s c", p=128)),
                    [(d_x2w, 16), (sp, gate_clsA_sp[0]), (ss, gate_clsA_ss[0]),
                     (sv, gate_clsA_sv[0]), (d_x2r, 16 * bt_)], d_x2r, 16)
        vch = ln_chain(slice(0, 64), DK, prevq, guard=stat_guard.get(0))
        sum_guard[0] = (sv, vch)
        knv_s = []
        knv = vch
        first_kn = [(sp, last_upd[0])]
        for s in range(NSA):
            a_b = bass.AP(aa, s * 16, [[96, 128], [1, 16], [0, 64]])
            b_b = bass.AP(bb, s * 16, [[96, 128], [1, 16], [0, 64]])
            knv = b.op("vector", lambda e, s=s, a_b=a_b: e.tensor_tensor(
                trash_v[:, :], k_sb[:, s], a_b, op=AL.mult),
                [(sv, knv), (sv, trash_tok[0])] + (first_kn if s == 0 else []), sv)
            knv = b.op("vector", lambda e, s=s, b_b=b_b: e.tensor_tensor(
                kn_sb[:, s], trash_v[:, :], b_b, op=AL.add), [(sv, knv)], sv)
            trash_tok[0] = knv
            knv_s.append(knv)
        stat_guard[0] = (sv, knv)

        kt_ev_s = []
        for s in range(NSA):
            ev = 0
            for half in range(2):
                j0 = 4 * half
                ri = rot_next()
                waits = [(sv, knv_s[s]), rot_state[ri]["free"]]
                for gi in range(4):
                    j = j0 + gi
                    m = b.op("tensor", lambda e, ri=ri, gi=gi, s=s, j=j: e.matmul(
                        rot[ri][:, gi * 128:(gi + 1) * 128], kn_sb[:, s, j * 128:(j + 1) * 128],
                        ident_sb[:, :], start=True, stop=True),
                        waits if gi == 0 else (), sp if gi == 3 else None)
                ev = b.op("scalar", lambda e, ri=ri, s=s, j0=j0: e.copy(
                    kt_sb[:, j0:j0 + 4, s * 128:(s + 1) * 128],
                    rot[ri][:, :].rearrange("p (g c) -> p g c", c=128)), [(sp, m)], ss)
                rot_state[ri]["free"] = (ss, ev)
            kt_ev_s.append(ev)

        pieces = a_ln_pieces(t + 1) if t + 1 < n_tiles_a else None

        def emit_scores(c):
            gcs = gscore[0]
            gscore[0] += 1
            cb = (gcs % 2) * 128
            war = (sp, upd_m_g.get(gcs - 2, 0))
            hw_masks = []
            for hw in range(2):
                sE = scE_ps if hw == 0 else rot[0]
                sO = scO_ps if hw == 0 else rot[1]
                first = True
                for jj in range(4 * hw, 4 * hw + 4):
                    he, ho = 2 * jj, 2 * jj + 1
                    col = (jj - 4 * hw) * 128
                    waits = [(ss, q_done), (ss, kt_ev_s[c]), (sv, last_sc_evac[hw])]
                    if hw == 1 and c == 0:
                        waits += [rot_state[0]["free"], rot_state[1]["free"]]
                    b.op("tensor", lambda e, he=he, col=col, c=c, sE=sE: e.matmul(
                        sE[:, col:col + 128], kt_sb[0:64, he // 2, c * 128:(c + 1) * 128],
                        q_sb[0:64, he // 2, c * 128:(c + 1) * 128],
                        start=True, stop=True, tile_position=(0, 0)),
                        waits if first else (), None)
                    first = False
                    m = b.op("tensor", lambda e, ho=ho, col=col, c=c, sO=sO: e.matmul(
                        sO[:, col:col + 128], kt_sb[64:128, ho // 2, c * 128:(c + 1) * 128],
                        q_sb[64:128, ho // 2, c * 128:(c + 1) * 128],
                        start=True, stop=True, tile_position=(64, 0)),
                        (), sp if jj == 4 * hw + 3 else None)
                mask_b = bass.AP(mask_sb, 0, [[128, 128], [0, 4], [1, 128]])
                ve = b.op("vector", lambda e, hw=hw, mask_b=mask_b, sE=sE, cb=cb: e.tensor_tensor(
                    sc_sb[:, 8 * hw:8 * hw + 8:2, cb:cb + 128],
                    sE[:, :].rearrange("p (j t) -> p j t", t=128), mask_b, op=AL.mult),
                    [(sp, m), war], sv)
                vo = b.op("vector", lambda e, hw=hw, mask_b=mask_b, sO=sO, cb=cb: e.tensor_tensor(
                    sc_sb[:, 8 * hw + 1:8 * hw + 8:2, cb:cb + 128],
                    sO[:, :].rearrange("p (j t) -> p j t", t=128), mask_b, op=AL.mult),
                    [(sv, ve)], sv)
                last_sc_evac[hw] = vo
                hw_masks.append(vo)
            return hw_masks, cb

        def emit_consumers(c, hw_masks, cb):
            gc = gchunk[0]
            for jj in range(8):
                he, ho = 2 * jj, 2 * jj + 1
                if jj == 0:
                    w0 = [(sv, hw_masks[0]), (ss, last_y_evac[0]), (ss, kv_v)]
                elif jj == 4:
                    w0 = [(sv, hw_masks[1])]
                else:
                    w0 = []
                stp = (gc == 0) and (jj in (3, 7))
                yp = y_psA if jj < 4 else y_psB
                col = (jj % 4) * 128
                b.op("tensor", lambda e, he=he, yp=yp, col=col, jj=jj, c=c, stp=stp, cb=cb: e.matmul(
                    yp[0:64, col:col + 128], v_sb[:, c, he * 64:(he + 1) * 64],
                    sc_sb[:, he, cb:cb + 128], start=(jj in (0, 4)), stop=stp,
                    tile_position=(0, 0)), w0, None)
                b.op("tensor", lambda e, ho=ho, yp=yp, col=col, jj=jj, c=c, stp=stp, cb=cb: e.matmul(
                    yp[64:128, col:col + 128], v_sb[:, c, ho * 64:(ho + 1) * 64],
                    sc_sb[:, ho, cb:cb + 128], start=(jj in (0, 4)), stop=stp,
                    tile_position=(0, 64), skip_group_check=True), (), None)
            if gc > 0:
                for jj in range(8):
                    he, ho = 2 * jj, 2 * jj + 1
                    w0 = [last_stcopy[0]] if jj == 0 else ()
                    stp = jj in (3, 7)
                    yp = y_psA if jj < 4 else y_psB
                    col = (jj % 4) * 128
                    b.op("tensor", lambda e, jj=jj, he=he, yp=yp, col=col, c=c, stp=stp: e.matmul(
                        yp[0:64, col:col + 128], st_sb[0:64, jj, :],
                        q_sb[0:64, he // 2, c * 128:(c + 1) * 128], start=False, stop=stp,
                        tile_position=(0, 0)), w0, None)
                    b.op("tensor", lambda e, jj=jj, ho=ho, yp=yp, col=col, c=c, stp=stp: e.matmul(
                        yp[64:128, col:col + 128], st_sb[64:128, jj, :],
                        q_sb[64:128, ho // 2, c * 128:(c + 1) * 128], start=False, stop=stp,
                        tile_position=(64, 64), skip_group_check=True), (), None)
            for jj in range(8):
                he, ho = 2 * jj, 2 * jj + 1
                w0 = [(sv, last_stcopy_v[0]), (ss, kv_v), (sv, knv)] if jj == 0 else ()
                b.op("tensor", lambda e, he=he, jj=jj, c=c, st=(jj == 0), fl=(jj == 7): e.matmul(
                    delta_ps[0:64, jj * 64:(jj + 1) * 64], kn_sb[:, c, he * 64:(he + 1) * 64],
                    v_sb[:, c, he * 64:(he + 1) * 64], start=st, stop=fl,
                    tile_position=(0, 0)), w0, None)
                m = b.op("tensor", lambda e, ho=ho, jj=jj, c=c, st=(jj == 0), fl=(jj == 7): e.matmul(
                    delta_ps[64:128, jj * 64:(jj + 1) * 64], kn_sb[:, c, ho * 64:(ho + 1) * 64],
                    v_sb[:, c, ho * 64:(ho + 1) * 64], start=st, stop=fl,
                    tile_position=(0, 64), skip_group_check=True), (), sp if jj == 7 else None)
            last_upd[0] = m
            last_inter[0] = m
            upd_m_g[gc] = m
            fold = b.op("vector", lambda e: e.scalar_tensor_tensor(
                stateF[:, :], delta_ps[:, :], 1.0, stateF[:, :],
                op0=AL.mult, op1=AL.add), [(sp, m)], sv)
            last_stcopy_v[0] = fold
            if t == n_tiles_a - 1:
                stc = b.op("vector", lambda e: e.tensor_copy(
                    st_sb[:, :, :], stateF[:, :].rearrange("p (j e) -> p j e", e=64)),
                    [(sv, fold)], sv)
                last_stcopy[0] = (sv, stc)
            else:
                stc = b.op("gpsimd", lambda e: e.tensor_copy(
                    st_sb[:, :, :], stateF[:, :].rearrange("p (j e) -> p j e", e=64)),
                    [(sv, fold)], sg)
                last_stcopy[0] = (sg, stc)
            b.op("scalar", lambda e, c=c: e.copy(
                y_sb[:, 0:4, c * 128:(c + 1) * 128],
                y_psA[:, :].rearrange("p (j t) -> p j t", t=128)), [(sp, m)], ss)
            last_y_evac[0] = b.op("scalar", lambda e, c=c: e.copy(
                y_sb[:, 4:8, c * 128:(c + 1) * 128],
                y_psB[:, :].rearrange("p (j t) -> p j t", t=128)), (), ss)
            y_ev_c[c] = last_y_evac[0]
            gchunk[0] += 1

        sc_cur = emit_scores(0)
        for c in range(NCH):
            if pieces is not None and c in (1, 2):
                (pieces[c - 1])()
            hw_masks, cb = sc_cur
            if c + 1 < NCH:
                sc_cur = emit_scores(c + 1)
            emit_consumers(c, hw_masks, cb)

        rot_state[0]["free"] = (sv, last_sc_evac[1])
        rot_state[1]["free"] = (sv, last_sc_evac[1])

        if pieces is not None:
            (pieces[2])()
            (pieces[3])()
        if t == n_tiles_a - 1:
            gate_clsB_sp[0] = sp.n
            gate_clsB_sv[0] = sv.n
            gate_clsB_ss[0] = ss.n

        base = 72 + 4 * par
        attn_ev = 0
        lastv = 0
        lasts = 0
        sg0 = [sum_guard.get(base, (sv, 0))]
        for s in range(NSA):
            for nn in range(2):
                ri = rot_next()
                waits = [(ss, y_ev_c[s]), (d_wa, 48), rot_state[ri]["free"]]
                for j in range(8):
                    m = b.op("tensor", lambda e, ri=ri, s=s, nn=nn, j=j: e.matmul(
                        rot[ri][:, 0:512], y_sb[:, j, s * 128:(s + 1) * 128],
                        wo_sb[:, j, nn * 512:(nn + 1) * 512], start=(j == 0), stop=(j == 7)),
                        waits if j == 0 else (), sp if j == 7 else None)
                attn_ev = b.op("scalar", lambda e, ri=ri, s=s, nn=nn: e.copy(
                    k_sb[:, s, nn * 512:(nn + 1) * 512], rot[ri][:, 0:512]),
                    [(sp, m), (sv, knv)], ss)
                rot_state[ri]["free"] = (ss, attn_ev)
                g2 = 2 * s + nn
                if pieces is not None and 2 <= g2 <= 5:
                    h1_pend = pieces[4](g2 - 2)
            lastv = b.op("vector", lambda e, s=s, base=base: e.tensor_scalar(
                trash_v[:, :], k_sb[:, s], 1.0, 0.0, op0=AL.mult, op1=AL.add,
                accum_out=sumx[:, base + s:base + s + 1]),
                [(ss, attn_ev), (sv, trash_tok[0])], sv)
            trash_tok[0] = lastv
            lasts = b.op("scalar", lambda e, s=s, base=base: e.activation(
                sq_sb[:, :], k_sb[:, s], AF.Square,
                accum_out=sumq[:, base + s:base + s + 1]),
                [(ss, attn_ev), (ss, sq_tok[0])] + (sg0 if s == 0 else []), ss)
            sq_tok[0] = lasts

        lnb0 = None
        if t == n_tiles_a - 1:
            gate_clsC_sp[0] = sp.n
            gate_clsC_ss[0] = ss.n
            gate_clsC_sv[0] = sv.n
            emit_wu_dmas()
            if n_tiles_b > 0:
                lnb0 = b_ln_pieces(0)
                lnb0[0]()
        b.op("vector", lambda e: None, [(ss, lasts)])
        vch = ln_chain(slice(base, base + NSA), D, lastv, guard=stat_guard.get(base))
        sum_guard[base] = (sv, vch)
        if lnb0 is not None:
            lnb0[1]()
            h2_pend0[0] = lnb0[2]()
        nrm = vch
        residg = 0
        rsem = sv
        reng = "vector"
        for s in range(NSA):
            nrm = b.op("vector", lambda e, s=s, base=base: e.tensor_scalar(
                k_sb[:, s], k_sb[:, s], aa[:, base + s:base + s + 1],
                bb[:, base + s:base + s + 1], op0=AL.mult, op1=AL.add), [(sv, nrm)], sv)
            residg = b.op(reng, lambda e, s=s, par=par: e.tensor_tensor(
                x_sb[:, par, s], x_sb[:, par, s], k_sb[:, s], op=AL.add), [(sv, nrm)], rsem)
        stat_guard[base] = (rsem, residg)
        attn_free[0] = (rsem, residg)
        b.op("sync", lambda e, t=t, par=par: e.dma_start(
            x2scr[t * TA:(t + 1) * TA, :].rearrange("(s p) c -> p s c", p=128),
            x_sb[:, par]), [(rsem, residg), (d_x2w, 16 * t)], d_x2w, 16)

    spA, svA, ssA, sgA = sp.n, sv.n, ss.n, sg.n

    pctxA.close()
    actxA.close()
    bctx = ExitStack()

    def sbb(name, shape, dt):
        return bctx.enter_context(nc.sbuf_tensor(name, shape, dt))

    wg_sb = sbb("wg_sb", [128, 8, HID], BF16)
    wtens["wg"] = wg_sb
    x2_sb = sbb("x2_sb", [128, 2, NSB, 1024], F32)
    wtens["x2"] = x2_sb
    h2_sb = sbb("h2_sb", [128, 2, NSB, 1024], BF16)
    wtens["h2"] = h2_sb
    h2t_sb = sbb("h2t_sb", [128, 2, 8, TB], BF16)
    wu_sb = sbb("wu_sb", [128, 8, HID], BF16)
    wtens["wu"] = wu_sb
    sg_sb = sbb("sg_sb", [128, 24, TB], BF16)
    f_sb = sbb("f_sb", [128, NSB, 1024], F32)
    wd_sb = sbb("wd_sb", [128, 24, 1024], BF16)

    pb = [bctx.enter_context(nc.psum_tensor(f"pb{i}", [128, 512], F32)) for i in range(8)]

    a_done = [(sp, spA), (sv, svA), (ss, ssA), (sg, sgA), (d_x2w, 16 * n_tiles_a)]
    b.op("gpsimd", lambda e: e.dma_start(
        wd_sb[:, :, 0:512], wdt[:, 0:512].rearrange("(j p) c -> p j c", p=128)),
        a_done + [(d_wb, 16 * wdma_n[0])], d_wb, 16)
    wd_rdy = 16 * (wdma_n[0] + 1)
    wdma_n[0] += 1
    b.op("gpsimd", lambda e: e.dma_start(
        wd_sb[:, :, 512:1024], wdt[:, 512:1024].rearrange("(j p) c -> p j c", p=128)),
        [(d_wb, 16 * wdma_n[0])], d_wb, 16)
    wd_rdy2 = 16 * (wdma_n[0] + 1)
    wdma_n[0] += 1

    stt_tok = {}
    bank_state = [{"free": (sv, 0), "free_s": (ss, 0)} for _ in range(4)] + \
        [{"free": (sv, gate_clsC_sv[0]), "free_s": (ss, gate_clsC_ss[0])} for _ in range(4)]
    blocksB = [(s, j) for s in range(NSB) for j in range(8)]

    def b_transpose(tt, h2_last):
        par = tt % 2
        ht_ev = 0
        for g in range(0, 16, 4):
            grp = blocksB[g:g + 4]
            ri = (g // 4) % 4
            waits = [(sv, h2_last), (d_ms, 32), bank_state[ri]["free"], bank_state[ri]["free_s"]]
            if tt == 0 and g == 0:
                waits.append((sp, gate_clsC_sp[0]))
                waits.append((ss, gate_clsC_ss[0]))
            for gi, (s, j) in enumerate(grp):
                m = b.op("tensor", lambda e, ri=ri, gi=gi, s=s, j=j, par=par: e.matmul(
                    pb[ri][:, gi * 128:(gi + 1) * 128], h2_sb[:, par, s, j * 128:(j + 1) * 128],
                    ident_sb[:, :], start=True, stop=True),
                    waits if gi == 0 else (), sp if gi == 3 else None)
            s0, j0 = grp[0]
            ev = b.op("vector", lambda e, ri=ri, s0=s0, j0=j0, par=par: e.tensor_copy(
                h2t_sb[:, par, j0:j0 + 4, s0 * 128:(s0 + 1) * 128],
                pb[ri][:, :].rearrange("p (g c) -> p g c", c=128)), [(sp, m)], sv)
            bank_state[ri]["free"] = (sv, ev)
            ht_ev = ev
        trans_m[par] = sp.n
        return ht_ev

    def make_fln(t, f_ev_s):
        par = t % 2
        base = 88 + 4 * par
        st = {}
        last = (t == n_tiles_b - 1)
        reng_b, rsem_b = ("vector", sv) if last else ("gpsimd", sg)

        if last:
            def p_last():
                sg0 = [sum_guard.get(base, (sv, 0))]
                for s in range(NSB):
                    lv = b.op("vector", lambda e, s=s: e.tensor_scalar(
                        trash_v[:, :], f_sb[:, s], 1.0, 0.0, op0=AL.mult, op1=AL.add,
                        accum_out=sumx[:, base + s:base + s + 1]),
                        [(ss, f_ev_s[s]), (sv, trash_tok[0])], sv)
                    trash_tok[0] = lv
                    lq = b.op("scalar", lambda e, s=s: e.activation(
                        sq_sb[:, :], f_sb[:, s], AF.Square,
                        accum_out=sumq[:, base + s:base + s + 1]),
                        [(ss, f_ev_s[s]), (ss, sq_tok[0])] + (sg0 if s == 0 else []), ss)
                    sq_tok[0] = lq
                    b.op("vector", lambda e: None, [(ss, lq)])
                    vch = ln_chain(slice(base + s, base + s + 1), D, lv)
                    nrm = b.op("vector", lambda e, s=s: e.tensor_scalar(
                        f_sb[:, s], f_sb[:, s], aa[:, base + s:base + s + 1],
                        bb[:, base + s:base + s + 1], op0=AL.mult, op1=AL.add),
                        [(sv, vch)], sv)
                    residg = b.op("vector", lambda e, s=s: e.tensor_tensor(
                        f_sb[:, s], x2_sb[:, par, s], f_sb[:, s], op=AL.add),
                        [(sv, nrm)], sv)
                    b.op("sync", lambda e, s=s: e.dma_start(
                        out_d[t * TB + s * 128:t * TB + (s + 1) * 128, :].rearrange(
                            "(s p) c -> p s c", p=128), f_sb[:, s:s + 1]),
                        [(sv, residg), (d_out, 16 * (t + s))], d_out, 16)

            def p_nop():
                pass
            return [p_nop, p_nop, p_last]

        def p0():
            lastv = 0
            for s in range(NSB):
                lastv = b.op("vector", lambda e, s=s: e.tensor_scalar(
                    trash_v[:, :], f_sb[:, s], 1.0, 0.0, op0=AL.mult, op1=AL.add,
                    accum_out=sumx[:, base + s:base + s + 1]),
                    [(ss, f_ev_s[s]), (sv, trash_tok[0])], sv)
                trash_tok[0] = lastv
            lasts = 0
            sg0 = [sum_guard.get(base, (sv, 0))]
            for s in range(NSB):
                lasts = b.op("scalar", lambda e, s=s: e.activation(
                    sq_sb[:, :], f_sb[:, s], AF.Square,
                    accum_out=sumq[:, base + s:base + s + 1]),
                    [(ss, f_ev_s[s]), (ss, sq_tok[0])] + (sg0 if s == 0 else []), ss)
                sq_tok[0] = lasts
            st["lastv"], st["lasts"] = lastv, lasts

        def p1():
            b.op("vector", lambda e: None, [(ss, st["lasts"])])
            vch = ln_chain(slice(base, base + NSB), D, st["lastv"])
            sum_guard[base] = (sv, vch)
            st["vch"] = vch

        def p2():
            nrm = st["vch"]
            residg = 0
            for s in range(NSB):
                nrm = b.op("vector", lambda e, s=s: e.tensor_scalar(
                    f_sb[:, s], f_sb[:, s], aa[:, base + s:base + s + 1],
                    bb[:, base + s:base + s + 1], op0=AL.mult, op1=AL.add), [(sv, nrm)], sv)
                residg = b.op(reng_b, lambda e, s=s: e.tensor_tensor(
                    f_sb[:, s], x2_sb[:, par, s], f_sb[:, s], op=AL.add), [(sv, nrm)], rsem_b)
            resid_done[0] = residg
            b.op("sync", lambda e: e.dma_start(
                out_d[t * TB:(t + 1) * TB, :].rearrange("(s p) c -> p s c", p=128),
                f_sb[:, :]), [(rsem_b, residg), (d_out, 16 * t)], d_out, 16)
        return [p0, p1, p2]

    if n_tiles_b > 0:
        ht_pend = b_transpose(0, h2_pend0[0])

    fln_pend = [None]
    for t in range(n_tiles_b):
        par = t % 2
        lnb = b_ln_pieces(t + 1) if t + 1 < n_tiles_b else None
        fp = fln_pend[0]

        h_ready = 0
        for mi in range(24):
            rg = (2 * mi) % 8
            ru = (2 * mi + 1) % 8
            gw = [(sv, ht_pend), (d_wb, wg_rdy[mi // 6]), bank_state[rg]["free"], bank_state[rg]["free_s"]]
            for k in range(8):
                mg = b.op("tensor", lambda e, rg=rg, mi=mi, k=k, par=par: e.matmul(
                    pb[rg][:, 0:TB], wg_sb[:, k, mi * 128:(mi + 1) * 128],
                    h2t_sb[:, par, k, :], start=(k == 0), stop=(k == 7)),
                    gw if k == 0 else (), sp if k == 7 else None)
            uw = [(d_wb, wu_rdy[mi // 12]), bank_state[ru]["free"], bank_state[ru]["free_s"]]
            for k in range(8):
                mu = b.op("tensor", lambda e, ru=ru, mi=mi, k=k, par=par: e.matmul(
                    pb[ru][:, 0:TB], wu_sb[:, k, mi * 128:(mi + 1) * 128],
                    h2t_sb[:, par, k, :], start=(k == 0), stop=(k == 7)),
                    uw if k == 0 else (), sp if k == 7 else None)
            e1 = b.op("scalar", lambda e, rg=rg, mi=mi: e.activation(
                sg_sb[:, mi, :], pb[rg][:, 0:TB], AF.Silu, scale=SC_FF),
                [(sp, mg)] + (a_done if (t == 0 and mi == 0) else []), ss)
            bank_state[rg]["free_s"] = (ss, e1)
            e2 = b.op("vector", lambda e, ru=ru, mi=mi: e.scalar_tensor_tensor(
                sg_sb[:, mi, :], pb[ru][:, 0:TB], 1.0, sg_sb[:, mi, :],
                op0=AL.mult, op1=AL.mult), [(sp, mu), (ss, e1)], sv)
            bank_state[ru]["free"] = (sv, e2)
            bank_state[rg]["free"] = (sv, e2)
            stt_tok[mi] = e2
            h_ready = e2
            if fp is not None:
                if mi == 1:
                    fp[0]()
                elif mi == 3:
                    fp[1]()
                elif mi == 5:
                    fp[2]()
            if lnb is not None:
                if mi == 13:
                    lnb[0]()
                elif mi == 15:
                    lnb[1]()
                elif mi == 17:
                    h2_next = lnb[2]()

        if lnb is not None:
            ht_pend = b_transpose(t + 1, h2_next)

        f_ev = 0
        f_ev_s = [0] * NSB
        for s in range(NSB):
            for nn in range(2):
                ri = 4 + 2 * s + nn
                wdw = (d_wb, wd_rdy) if nn == 0 else (d_wb, wd_rdy2)
                waits = [wdw, bank_state[ri]["free"], bank_state[ri]["free_s"]]
                for kh in range(24):
                    m = b.op("tensor", lambda e, ri=ri, s=s, nn=nn, kh=kh: e.matmul(
                        pb[ri][:, 0:512], sg_sb[:, kh, s * 128:(s + 1) * 128],
                        wd_sb[:, kh, nn * 512:(nn + 1) * 512], start=(kh == 0), stop=(kh == 23)),
                        (waits if kh == 0 else []) + [(sv, stt_tok[kh])], sp if kh == 23 else None)
                f_ev = b.op("scalar", lambda e, ri=ri, s=s, nn=nn: e.activation(
                    f_sb[:, s, nn * 512:(nn + 1) * 512], pb[ri][:, 0:512],
                    AF.Identity, scale=SC_DN), [(sp, m), (d_out, 16 * t)], ss)
                bank_state[ri]["free_s"] = (ss, f_ev)
            f_ev_s[s] = f_ev

        fln_pend[0] = make_fln(t, f_ev_s)

    if fln_pend[0] is not None:
        fp = fln_pend[0]
        fp[0]()
        fp[1]()
        fp[2]()

    finals = [(d_out, 16 * (n_tiles_b + 1)), (d_x2r, 16 * n_tiles_b), (d_x2w, 16 * n_tiles_a),
              (d_x, 16 * (n_tiles_a - 1)), (d_x0, 16 * NSA), (d_wa, 48), (d_wb, 128),
              (d_ms, 32)]
    b.op("sync", lambda e: None, [(s, v) for s, v in finals if v > 0])

    emit(nc, b, ctx)
    bctx.close()
    ctx.close()
    return nc


_mask_np = np.triu(np.ones((128, 128), dtype=np.float32)).astype(ml_dtypes.bfloat16)
_ident_np = np.eye(128, dtype=np.float32).astype(ml_dtypes.bfloat16)


def kernel(L0_attn_x, L0_attn_W_qkv, L0_attn_W_o, L0_ffn_w_gate, L0_ffn_w_up, L0_ffn_w_down):
    x = np.ascontiguousarray(np.asarray(L0_attn_x, dtype=np.float32))
    wqkv = np.ascontiguousarray(np.asarray(L0_attn_W_qkv, dtype=np.float32))
    wo = np.ascontiguousarray(np.asarray(L0_attn_W_o, dtype=np.float32))
    wgt = np.ascontiguousarray(np.asarray(L0_ffn_w_gate, dtype=np.int32).T)
    wut = np.ascontiguousarray(np.asarray(L0_ffn_w_up, dtype=np.int32).T)
    wdt = np.ascontiguousarray(np.asarray(L0_ffn_w_down, dtype=np.int32).T)

    nc = build()
    in_maps = []
    for bt in range(8):
        in_maps.append({
            "x_in": x[bt], "wqkv": wqkv, "wo": wo,
            "wgt": wgt, "wut": wut, "wdt": wdt,
            "maskd": _mask_np, "identd": _ident_np,
        })
    res = run_bass_kernel_spmd(nc, in_maps, core_ids=list(range(8)))
    return np.stack([res.results[i]["out"] for i in range(8)], axis=0)


# version 33
# speedup vs baseline: 1.0091x; 1.0091x over previous
import numpy as np
import ml_dtypes
from contextlib import ExitStack

import concourse.bass as bass
import concourse.mybir as mybir
from concourse.bass_utils import run_bass_kernel_spmd

F32 = mybir.dt.float32
BF16 = mybir.dt.bfloat16
I32 = mybir.dt.int32
AL = mybir.AluOpType
AF = mybir.ActivationFunctionType
AX = mybir.AxisListType

D = 1024
H = 16
DK = 64
L = 4096
HID = 3072
EPS = 1e-5
CH = 128
TA = 512
TB = 256
NTA = L // TA
NTB = L // TB
NSA = TA // 128
NSB = TB // 128
NCH = TA // CH
MAGIC = 0x5F3759DF
SC_FF = 1.0 / 127.0
SC_DN = 1.0 / (127.0 * 127.0)

ENGINES = ("sync", "tensor", "vector", "scalar", "gpsimd")


class Sem:
    __slots__ = ("name", "n", "h")

    def __init__(self, name):
        self.name = name
        self.n = 0
        self.h = None


class B:

    FAST = True

    def __init__(self):
        self.ops = {e: [] for e in ENGINES}
        self.sems = []
        self.auto = {}

    def sem(self, name):
        s = Sem(name)
        self.sems.append(s)
        return s

    def op(self, eng, fn, waits=(), inc=None, amt=1):
        w = [(s, v) for (s, v) in waits if v > 0]
        a = self.auto.get(eng)
        if a is not None and a.n > 0 and not B.FAST:
            w.append((a, a.n))
        self.ops[eng].append((w, fn, inc, amt))
        if inc is not None:
            inc.n += amt
            return inc.n
        return None


def emit(nc, b, ctx):
    for s in b.sems:
        s.h = ctx.enter_context(nc.semaphore(s.name))
    final = {id(s): s.n for s in b.sems}
    for e in ENGINES:
        for w, _, _, _ in b.ops[e]:
            for s, v in w:
                assert v <= final[id(s)], \
                    f"wait {s.name}>={v} unreachable (final {final[id(s)]})"
    block = ctx.enter_context(nc.Block())

    def run(eng, ops):
        seen = {}
        for w, fn, inc, amt in ops:
            for s, v in w:
                if seen.get(id(s), -1) < v:
                    eng.wait_ge(s.h, v)
                    seen[id(s)] = v
            ins = fn(eng)
            if inc is not None:
                assert ins is not None
                ins.then_inc(inc.h, amt)

    block.sync(lambda e: run(e, b.ops["sync"]))
    block.tensor(lambda e: run(e, b.ops["tensor"]))
    block.vector(lambda e: run(e, b.ops["vector"]))
    block.scalar(lambda e: run(e, b.ops["scalar"]))
    block.gpsimd(lambda e: run(e, b.ops["gpsimd"]))


def build(n_tiles_a=NTA, n_tiles_b=NTB):
    nc = bass.Bass("TRN2", target_bir_lowering=False)

    x_in = nc.dram_tensor("x_in", [L, D], F32, kind="ExternalInput").ap()
    wqkv = nc.dram_tensor("wqkv", [D, 3 * D], F32, kind="ExternalInput").ap()
    wo = nc.dram_tensor("wo", [D, D], F32, kind="ExternalInput").ap()
    wgt = nc.dram_tensor("wgt", [D, HID], I32, kind="ExternalInput").ap()
    wut = nc.dram_tensor("wut", [D, HID], I32, kind="ExternalInput").ap()
    wdt = nc.dram_tensor("wdt", [HID, D], I32, kind="ExternalInput").ap()
    maskd = nc.dram_tensor("maskd", [128, 128], BF16, kind="ExternalInput").ap()
    identd = nc.dram_tensor("identd", [128, 128], BF16, kind="ExternalInput").ap()
    out_d = nc.dram_tensor("out", [L, D], F32, kind="ExternalOutput").ap()
    x2scr = nc.dram_tensor("x2scr", [L, D], F32).ap()

    b = B()
    d_x0 = b.sem("d_x0")
    d_x = b.sem("d_x")
    d_wa = b.sem("d_wa")
    d_wb = b.sem("d_wb")
    d_ms = b.sem("d_ms")
    d_x2w = b.sem("d_x2w")
    d_x2r = b.sem("d_x2r")
    d_out = b.sem("d_out")
    sp = b.sem("sp")
    sv = b.sem("sv")
    ss = b.sem("ss")
    sg = b.sem("sg")
    b.auto = {"vector": sv, "scalar": ss, "gpsimd": sg}

    ctx = ExitStack()

    def sb(name, shape, dt):
        return ctx.enter_context(nc.sbuf_tensor(name, shape, dt))

    mask_sb = sb("mask_sb", [128, 128], BF16)
    ident_sb = sb("ident_sb", [128, 128], BF16)
    magic_sb = sb("magic_sb", [128, 64], I32)
    sumx = sb("sumx", [128, 96], F32)
    sumq = sb("sumq", [128, 96], F32)
    aa = sb("aa", [128, 96], F32)
    bb = sb("bb", [128, 96], F32)
    c1 = sb("c1", [128, 64], F32)
    c2 = sb("c2", [128, 64], F32)
    c4 = sb("c4", [128, 64], F32)
    ish = sb("ish", [128, 64], I32)
    trash_v = sb("trash_v", [128, 1024], F32)
    sq_sb = sb("sq_sb", [128, 1024], F32)

    aa_guard = [(ss, 0)]
    ln_v = [0]

    def ln_chain(sl, N, last_v, guard=None):
        def w(v):
            return [(sv, v)]
        z = slice(0, sl.stop - sl.start)
        v = last_v
        v = b.op("vector", lambda e, s=sl, z=z: e.tensor_scalar_mul(c1[:, z], sumx[:, s], 1.0 / N),
                 w(v) + [guard if guard is not None else aa_guard[0]], sv)
        v = b.op("vector", lambda e, z=z: e.tensor_mul(c4[:, z], c1[:, z], c1[:, z]), w(v), sv)
        v = b.op("vector", lambda e, s=sl, z=z: e.scalar_tensor_tensor(
            c2[:, z], sumq[:, s], 1.0 / N, c4[:, z], op0=AL.mult, op1=AL.subtract), w(v), sv)
        v = b.op("vector", lambda e, z=z: e.tensor_scalar_add(c2[:, z], c2[:, z], EPS), w(v), sv)
        v = b.op("vector", lambda e, z=z: e.tensor_scalar(
            ish[:, z], c2[:, z].bitcast(I32), 1, None, op0=AL.arith_shift_right), w(v), sv)
        v = b.op("vector", lambda e, s=sl, z=z: e.tensor_tensor(
            aa[:, s].bitcast(I32), magic_sb[:, z], ish[:, z], op=AL.subtract), w(v), sv)
        for _ in range(3):
            v = b.op("vector", lambda e, s=sl, z=z: e.tensor_mul(c4[:, z], aa[:, s], aa[:, s]), w(v), sv)
            v = b.op("vector", lambda e, z=z: e.scalar_tensor_tensor(
                c4[:, z], c4[:, z], -0.5, c2[:, z], op0=AL.mult, op1=AL.mult), w(v), sv)
            v = b.op("vector", lambda e, s=sl, z=z: e.scalar_tensor_tensor(
                aa[:, s], c4[:, z], 1.5, aa[:, s], op0=AL.add, op1=AL.mult), w(v), sv)
        v = b.op("vector", lambda e, s=sl, z=z: e.scalar_tensor_tensor(
            bb[:, s], c1[:, z], -1.0, aa[:, s], op0=AL.mult, op1=AL.mult), w(v), sv)
        return v

    def ln_chain_part1(sl, N, last_v, guard=None, swait=None):
        def w(v):
            return [(sv, v)]
        z = slice(0, sl.stop - sl.start)
        w0 = w(last_v) + ([guard] if guard is not None else [])
        if swait is not None:
            w0.append(swait)
        v = b.op("vector", lambda e, s=sl, z=z: e.tensor_scalar_mul(c1[:, z], sumx[:, s], 1.0 / N),
                 w0, sv)
        v = b.op("vector", lambda e, z=z: e.tensor_mul(c4[:, z], c1[:, z], c1[:, z]), w(v), sv)
        v = b.op("vector", lambda e, s=sl, z=z: e.scalar_tensor_tensor(
            c2[:, z], sumq[:, s], 1.0 / N, c4[:, z], op0=AL.mult, op1=AL.subtract), w(v), sv)
        v = b.op("vector", lambda e, z=z: e.tensor_scalar_add(c2[:, z], c2[:, z], EPS), w(v), sv)
        v = b.op("vector", lambda e, z=z: e.tensor_scalar(
            ish[:, z], c2[:, z].bitcast(I32), 1, None, op0=AL.arith_shift_right), w(v), sv)
        v = b.op("vector", lambda e, s=sl, z=z: e.tensor_tensor(
            aa[:, s].bitcast(I32), magic_sb[:, z], ish[:, z], op=AL.subtract), w(v), sv)
        return v

    def ln_chain_part2(sl, last_v):
        def w(v):
            return [(sv, v)]
        z = slice(0, sl.stop - sl.start)
        v = last_v
        for _ in range(3):
            v = b.op("vector", lambda e, s=sl, z=z: e.tensor_mul(c4[:, z], aa[:, s], aa[:, s]), w(v), sv)
            v = b.op("vector", lambda e, z=z: e.scalar_tensor_tensor(
                c4[:, z], c4[:, z], -0.5, c2[:, z], op0=AL.mult, op1=AL.mult), w(v), sv)
            v = b.op("vector", lambda e, s=sl, z=z: e.scalar_tensor_tensor(
                aa[:, s], c4[:, z], 1.5, aa[:, s], op0=AL.add, op1=AL.mult), w(v), sv)
        v = b.op("vector", lambda e, s=sl, z=z: e.scalar_tensor_tensor(
            bb[:, s], c1[:, z], -1.0, aa[:, s], op0=AL.mult, op1=AL.mult), w(v), sv)
        return v

    actxA = ExitStack()

    def sba(name, shape, dt):
        return actxA.enter_context(nc.sbuf_tensor(name, shape, dt))

    wq_sb = sba("wq_sb", [128, 8, 1024], BF16)
    wkv_sb = sba("wkv_sb", [128, 8, 2048], BF16)
    h1_sb = sba("h1_sb", [128, NSA, 1024], BF16)
    h1t_sb = sba("h1t_sb", [128, 8, TA], BF16)
    ksq_sb = sba("ksq_sb", [128, 1024], F32)
    ksq2_sb = sba("ksq2_sb", [128, 1024], F32)
    q_sb = sba("q_sb", [128, 8, TA], BF16)
    kn_sb = sba("kn_sb", [128, NSA, 1024], BF16)
    kt_sb = sba("kt_sb", [128, 8, TA], BF16)
    v_sb = sba("v_sb", [128, NSA, 1024], BF16)
    sc_sb = sba("sc_sb", [128, 16, 256], BF16)
    scr_sb = sba("scr_sb", [128, 2, 512], BF16)
    st_sb = sba("st_sb", [128, 8, 64], BF16)
    stateF = sba("stateF", [128, 512], F32)
    y_sb = sba("y_sb", [128, 8, TA], BF16)
    wo_sb = sba("wo_sb", [128, 8, 1024], BF16)
    k_sb = sba("k_sb", [128, NSA, 1024], F32)
    x_sb = sba("x_sb", [128, 2, NSA, 1024], F32)

    pctxA = ExitStack()

    def psa(name, shape):
        return pctxA.enter_context(nc.psum_tensor(name, shape, F32))

    delta_ps = psa("delta_ps", [128, 512])
    y_psA = psa("y_psA", [128, 512])
    y_psB = psa("y_psB", [128, 512])
    scE_ps = psa("scE_ps", [128, 512])
    scO_ps = psa("scO_ps", [128, 512])
    rot = [psa(f"rot{i}", [128, 512]) for i in range(3)]

    b.op("vector", lambda e: e.memset(magic_sb[:, :], MAGIC), (), sv)
    b.op("vector", lambda e: e.memset(stateF[:, :], 0.0), (), sv)
    for s0_ in range(NSA):
        b.op("sync", lambda e, s=s0_: e.dma_start(
            x_sb[:, 0, s:s + 1], x_in[s * 128:(s + 1) * 128, :].rearrange(
                "(s p) c -> p s c", p=128)), (), d_x0, 16)
    b.op("sync", lambda e: e.dma_start(mask_sb[:, :], maskd), (), d_ms, 16)
    b.op("sync", lambda e: e.dma_start(ident_sb[:, :], identd), [(d_ms, 16)], d_ms, 16)
    b.op("scalar", lambda e: e.preload_act_table(AF.Silu), (), None)
    b.op("gpsimd", lambda e: e.dma_start(
        wq_sb[:, :, :], wqkv[:, 0:1024].rearrange("(j p) c -> p j c", p=128)), (), d_wa, 16)
    b.op("gpsimd", lambda e: e.dma_start(
        wkv_sb[:, :, :], wqkv[:, 1024:3072].rearrange("(j p) c -> p j c", p=128)),
        [(d_wa, 16)], d_wa, 16)
    b.op("gpsimd", lambda e: e.dma_start(
        wo_sb[:, :, :], wo.rearrange("(j p) c -> p j c", p=128)), [(d_wa, 32)], d_wa, 16)

    rot_state = [{"free": (sv, 0)} for _ in range(3)]
    rot_i = [0]

    def rot_next():
        i = rot_i[0] % 3
        rot_i[0] += 1
        return i

    gate_clsA_sp = [0]
    gate_clsA_sv = [0]
    gate_clsA_ss = [0]
    wdma_n = [0]
    wg_rdy = {}
    wu_rdy = {}
    wtens = {}

    def emit_wg_dmas():
        for ci in range(4):
            w0 = ([(sp, gate_clsA_sp[0]), (sv, gate_clsA_sv[0]),
                   (ss, gate_clsA_ss[0]), (d_wa, 48)]
                  if wdma_n[0] == 0 else [(d_wb, 16 * wdma_n[0])])
            b.op("gpsimd", lambda e, ci=ci: e.dma_start(
                wtens["wg"][:, :, ci * 768:(ci + 1) * 768],
                wgt[:, ci * 768:(ci + 1) * 768].rearrange("(j p) c -> p j c", p=128)),
                w0, d_wb, 16)
            wg_rdy[ci] = 16 * (wdma_n[0] + 1)
            wdma_n[0] += 1

    def emit_wu_dmas():
        for ci in range(2):
            w0 = [(sp, gate_clsC_sp[0]), (ss, gate_clsC_ss[0]),
                  (sv, gate_clsC_sv[0]), last_stcopy[0],
                  (d_wb, 16 * wdma_n[0])]
            b.op("gpsimd", lambda e, ci=ci: e.dma_start(
                wtens["wu"][:, :, ci * 1536:(ci + 1) * 1536],
                wut[:, ci * 1536:(ci + 1) * 1536].rearrange("(j p) c -> p j c", p=128)),
                w0, d_wb, 16)
            wu_rdy[ci] = 16 * (wdma_n[0] + 1)
            wdma_n[0] += 1
    gate_clsB_sp = [0]
    gate_clsB_sv = [0]
    gate_clsB_ss = [0]
    gate_clsC_sp = [0]
    gate_clsC_ss = [0]
    gate_clsC_sv = [0]
    last_upd = [0]
    gscore = [0]
    upd_m_g = {}
    trash_tok = [0]
    sq_tok = [0]
    last_sc_evac = [0, 0]
    last_mask_v = [0]
    last_y_evac = [0]
    y_ev_c = [0, 0, 0, 0]
    last_inter = [0]
    last_stcopy = [(sg, 0)]
    last_stcopy_v = [0]
    h1_read_m = {}
    attn_free = [(sv, 0)]
    stat_guard = {}
    sum_guard = {}
    gchunk = [0]
    blocks = [(s, j) for s in range(NSA) for j in range(8)]

    def a_ln_pieces(tt):
        par = tt % 2
        base = 64 + 4 * par
        st_ = {}

        def xw(s):
            return (d_x0, 16 * (s + 1)) if tt == 0 else (d_x, 16 * tt)

        def p0a():
            lastv = 0
            for s in (0, 1):
                lastv = b.op("vector", lambda e, s=s, par=par, base=base: e.tensor_scalar(
                    trash_v[:, :], x_sb[:, par, s], 1.0, 0.0, op0=AL.mult, op1=AL.add,
                    accum_out=sumx[:, base + s:base + s + 1]),
                    [xw(s), sum_guard.get(base, (sv, 0)), (sv, trash_tok[0])], sv)
                trash_tok[0] = lastv

        def p0b():
            lastv = 0
            for s in (2, 3):
                lastv = b.op("vector", lambda e, s=s, par=par, base=base: e.tensor_scalar(
                    trash_v[:, :], x_sb[:, par, s], 1.0, 0.0, op0=AL.mult, op1=AL.add,
                    accum_out=sumx[:, base + s:base + s + 1]),
                    [xw(s), sum_guard.get(base, (sv, 0)), (sv, trash_tok[0])], sv)
                trash_tok[0] = lastv
            lasts = 0
            for s in range(NSA):
                lasts = b.op("scalar", lambda e, s=s, par=par, base=base: e.activation(
                    sq_sb[:, :], x_sb[:, par, s], AF.Square,
                    accum_out=sumq[:, base + s:base + s + 1]),
                    [xw(s), sum_guard.get(base, (sv, 0)), (ss, sq_tok[0])], ss)
                sq_tok[0] = lasts
            st_["v"] = lastv
            st_["s"] = lasts

        def p1():
            st_["c1"] = ln_chain_part1(slice(base, base + NSA), D, st_["v"],
                                       guard=stat_guard.get(base), swait=(ss, st_["s"]))

        def p2a():
            vch = ln_chain_part2(slice(base, base + NSA), st_["c1"])
            sum_guard[base] = (sv, vch)
            st_["vch"] = vch

        def p2b(s):
            h1_last = b.op("scalar", lambda e, s=s, par=par, base=base: e.activation(
                h1_sb[:, s], x_sb[:, par, s], AF.Identity,
                bias=bb[:, base + s:base + s + 1], scale=aa[:, base + s:base + s + 1]),
                [(sv, st_["vch"]), (sp, h1_read_m.get(0, 0))], ss)
            if s == NSA - 1:
                stat_guard[base] = (ss, h1_last)
            return h1_last
        return p0a, p0b, p1, p2a, p2b

    def a_ln(tt):
        p0a, p0b, p1, p2a, p2b = a_ln_pieces(tt)
        p0a(); p0b(); p1(); p2a()
        h1_last = 0
        h1_s = []
        for s in range(NSA):
            h1_last = p2b(s)
            h1_s.append(h1_last)
        return h1_last, h1_s

    trans_m = {}
    resid_done = [0]
    h2_pend0 = [0]

    def b_ln_pieces(tt):
        parB = tt % 2
        baseB = 80 + 4 * parB
        stB = {}

        def p0():
            if tt >= 2:
                waits = [(d_x2w, 16 * (tt * TB // TA + 1)), (d_x2r, 16 * tt),
                         (sg, resid_done[0])]
                b.op("sync", lambda e: e.dma_start(
                    wtens["x2"][:, parB], x2scr[tt * TB:(tt + 1) * TB, :].rearrange(
                        "(s p) c -> p s c", p=128)), waits, d_x2r, 16)
            x2r = 16 * (tt + 1)
            lastv = 0
            for s in range(NSB):
                lastv = b.op("vector", lambda e, s=s: e.tensor_scalar(
                    trash_v[:, :], wtens["x2"][:, parB, s], 1.0, 0.0, op0=AL.mult, op1=AL.add,
                    accum_out=sumx[:, baseB + s:baseB + s + 1]),
                    [(d_x2r, x2r), (sv, trash_tok[0])], sv)
                trash_tok[0] = lastv
            lasts = 0
            sg0 = [sum_guard.get(baseB, (sv, 0))]
            for s in range(NSB):
                lasts = b.op("scalar", lambda e, s=s: e.activation(
                    sq_sb[:, :], wtens["x2"][:, parB, s], AF.Square,
                    accum_out=sumq[:, baseB + s:baseB + s + 1]),
                    [(d_x2r, x2r), (ss, sq_tok[0])] + (sg0 if s == 0 else []), ss)
                sq_tok[0] = lasts
            stB["lastv"], stB["lasts"] = lastv, lasts

        def p1():
            b.op("vector", lambda e: None, [(ss, stB["lasts"])])
            vch = ln_chain(slice(baseB, baseB + NSB), D, stB["lastv"])
            sum_guard[baseB] = (sv, vch)
            stB["vch"] = vch

        def p2():
            h2_last = 0
            for s in range(NSB):
                h2_last = b.op("vector", lambda e, s=s: e.tensor_scalar(
                    wtens["h2"][:, parB, s], wtens["x2"][:, parB, s],
                    aa[:, baseB + s:baseB + s + 1],
                    bb[:, baseB + s:baseB + s + 1], op0=AL.mult, op1=AL.add),
                    [(sv, stB["vch"]), (sp, trans_m.get(parB, 0))] +
                    ([(sp, gate_clsB_sp[0])] if tt == 0 else []), sv)
            return h2_last
        return [p0, p1, p2]

    h1_pend, h1_pend_s = (a_ln(0) if n_tiles_a > 0 else (0, [0] * NSA))

    for t in range(n_tiles_a):
        par = t % 2
        if t + 1 < n_tiles_a:
            tt = t + 1
            waits = [(d_x, 16 * (tt - 1))]
            if tt >= 2:
                waits.append((d_x2w, 16 * (tt - 1)))
            b.op("sync", lambda e, tt=tt: e.dma_start(
                x_sb[:, tt % 2], x_in[tt * TA:(tt + 1) * TA, :].rearrange("(s p) c -> p s c", p=128)),
                waits, d_x, 16)

        h1t_ready = 0
        for g in range(0, len(blocks), 4):
            grp = blocks[g:g + 4]
            s0, j0 = grp[0]
            ri = rot_next()
            hw_ = h1_pend_s[s0] if t == 0 else h1_pend
            waits = [(ss, hw_), (d_ms, 32), rot_state[ri]["free"]]
            for gi, (s, j) in enumerate(grp):
                m = b.op("tensor", lambda e, ri=ri, gi=gi, s=s, j=j, par=par: e.matmul(
                    rot[ri][:, gi * 128:(gi + 1) * 128], h1_sb[:, s, j * 128:(j + 1) * 128],
                    ident_sb[:, :], start=True, stop=True),
                    waits if gi == 0 else (), sp if gi == 3 else None)
            h1t_ready = b.op("scalar", lambda e, ri=ri, s0=s0, j0=j0: e.copy(
                h1t_sb[:, j0:j0 + 4, s0 * 128:(s0 + 1) * 128],
                rot[ri][:, :].rearrange("p (g c) -> p g c", c=128)), [(sp, m)], ss)
            rot_state[ri]["free"] = (ss, h1t_ready)
        h1_read_m[0] = sp.n

        q_done = 0
        for mi in range(8):
            ri = rot_next()
            waits = [(ss, h1t_ready), (d_wa, 16), rot_state[ri]["free"]]
            for k in range(8):
                m = b.op("tensor", lambda e, ri=ri, mi=mi, k=k: e.matmul(
                    rot[ri][:, 0:512], wq_sb[:, k, mi * 128:(mi + 1) * 128],
                    h1t_sb[:, k, :], start=(k == 0), stop=(k == 7)),
                    waits if k == 0 else (), sp if k == 7 else None)
            q_done = b.op("scalar", lambda e, ri=ri, mi=mi: e.copy(
                q_sb[:, mi, :], rot[ri][:, 0:512]), [(sp, m)], ss)
            rot_state[ri]["free"] = (ss, q_done)

        kv_k = 0
        kv_v = 0
        first_kev = True
        for nn in (0, 1):
            for s in range(NSA):
                ri = rot_next()
                waits = [(ss, h1t_ready), (d_wa, 32), rot_state[ri]["free"]]
                for k in range(8):
                    m = b.op("tensor", lambda e, ri=ri, s=s, nn=nn, k=k: e.matmul(
                        rot[ri][:, 0:512], h1t_sb[:, k, s * 128:(s + 1) * 128],
                        wkv_sb[:, k, nn * 512:(nn + 1) * 512], start=(k == 0), stop=(k == 7)),
                        waits if k == 0 else (), sp if k == 7 else None)
                ew = [(sp, m)]
                if first_kev:
                    ew.append(attn_free[0])
                    first_kev = False
                kv_k = b.op("scalar", lambda e, ri=ri, s=s, nn=nn: e.copy(
                    k_sb[:, s, nn * 512:(nn + 1) * 512], rot[ri][:, 0:512]), ew, ss)
                rot_state[ri]["free"] = (ss, kv_k)
        lastv = 0
        for s in range(NSA):
            lastv = b.op("vector", lambda e, s=s: e.reduce_sum(
                sumx[:, s * 16:(s + 1) * 16],
                k_sb[:, s].rearrange("p (h e) -> p h e", e=64), axis=AX.X),
                [(ss, kv_k)] + ([sum_guard.get(0, (sv, 0))] if s == 0 else []), sv)
        prevq = lastv
        redq = [0, 0]
        for s in range(NSA):
            buf_k = ksq_sb if s % 2 == 0 else ksq2_sb
            sx = b.op("scalar", lambda e, s=s, buf_k=buf_k: e.activation(
                buf_k[:, :], k_sb[:, s], AF.Square),
                [(ss, kv_k), (sv, redq[s % 2])], ss)
            prevq = b.op("vector", lambda e, s=s, buf_k=buf_k: e.reduce_sum(
                sumq[:, s * 16:(s + 1) * 16],
                buf_k[:, :].rearrange("p (h e) -> p h e", e=64), axis=AX.X),
                [(ss, sx)], sv)
            redq[s % 2] = prevq
        for nn in (2, 3):
            for s in range(NSA):
                ri = rot_next()
                waits = [(ss, h1t_ready), (d_wa, 32), rot_state[ri]["free"]]
                for k in range(8):
                    m = b.op("tensor", lambda e, ri=ri, s=s, nn=nn, k=k: e.matmul(
                        rot[ri][:, 0:512], h1t_sb[:, k, s * 128:(s + 1) * 128],
                        wkv_sb[:, k, nn * 512:(nn + 1) * 512], start=(k == 0), stop=(k == 7)),
                        waits if k == 0 else (), sp if k == 7 else None)
                kv_v = b.op("scalar", lambda e, ri=ri, s=s, nn=nn: e.copy(
                    v_sb[:, s, (nn - 2) * 512:(nn - 1) * 512], rot[ri][:, 0:512]), [(sp, m)], ss)
                rot_state[ri]["free"] = (ss, kv_v)
        if t == n_tiles_a - 1:
            gate_clsA_sp[0] = sp.n
            gate_clsA_ss[0] = ss.n
            gate_clsA_sv[0] = sv.n
            emit_wg_dmas()
            for bt_ in range(2):
                b.op("sync", lambda e, bt=bt_: e.dma_start(
                    wtens["x2"][:, bt], x2scr[bt * TB:(bt + 1) * TB, :].rearrange(
                        "(s p) c -> p s c", p=128)),
                    [(d_x2w, 16), (sp, gate_clsA_sp[0]), (ss, gate_clsA_ss[0]),
                     (sv, gate_clsA_sv[0]), (d_x2r, 16 * bt_)], d_x2r, 16)
        vch = ln_chain(slice(0, 64), DK, prevq, guard=stat_guard.get(0))
        sum_guard[0] = (sv, vch)
        knv_s = []
        knv = vch
        first_kn = [(sp, last_upd[0])]
        for s in range(NSA):
            a_b = bass.AP(aa, s * 16, [[96, 128], [1, 16], [0, 64]])
            b_b = bass.AP(bb, s * 16, [[96, 128], [1, 16], [0, 64]])
            knv = b.op("vector", lambda e, s=s, a_b=a_b: e.tensor_tensor(
                trash_v[:, :], k_sb[:, s], a_b, op=AL.mult),
                [(sv, knv), (sv, trash_tok[0])] + (first_kn if s == 0 else []), sv)
            knv = b.op("vector", lambda e, s=s, b_b=b_b: e.tensor_tensor(
                kn_sb[:, s], trash_v[:, :], b_b, op=AL.add), [(sv, knv)], sv)
            trash_tok[0] = knv
            knv_s.append(knv)
        stat_guard[0] = (sv, knv)

        kt_ev_s = []
        for s in range(NSA):
            ev = 0
            for half in range(2):
                j0 = 4 * half
                ri = rot_next()
                waits = [(sv, knv_s[s]), rot_state[ri]["free"]]
                for gi in range(4):
                    j = j0 + gi
                    m = b.op("tensor", lambda e, ri=ri, gi=gi, s=s, j=j: e.matmul(
                        rot[ri][:, gi * 128:(gi + 1) * 128], kn_sb[:, s, j * 128:(j + 1) * 128],
                        ident_sb[:, :], start=True, stop=True),
                        waits if gi == 0 else (), sp if gi == 3 else None)
                ev = b.op("scalar", lambda e, ri=ri, s=s, j0=j0: e.copy(
                    kt_sb[:, j0:j0 + 4, s * 128:(s + 1) * 128],
                    rot[ri][:, :].rearrange("p (g c) -> p g c", c=128)), [(sp, m)], ss)
                rot_state[ri]["free"] = (ss, ev)
            kt_ev_s.append(ev)

        pieces = a_ln_pieces(t + 1) if t + 1 < n_tiles_a else None

        def emit_scores(c):
            gcs = gscore[0]
            gscore[0] += 1
            cb = (gcs % 2) * 128
            war = (sp, upd_m_g.get(gcs - 2, 0))
            hw_masks = []
            for hw in range(2):
                sE = scE_ps if hw == 0 else rot[0]
                sO = scO_ps if hw == 0 else rot[1]
                first = True
                for jj in range(4 * hw, 4 * hw + 4):
                    he, ho = 2 * jj, 2 * jj + 1
                    col = (jj - 4 * hw) * 128
                    waits = [(ss, q_done), (ss, kt_ev_s[c]), (sv, last_sc_evac[hw])]
                    if hw == 1 and c == 0:
                        waits += [rot_state[0]["free"], rot_state[1]["free"]]
                    b.op("tensor", lambda e, he=he, col=col, c=c, sE=sE: e.matmul(
                        sE[:, col:col + 128], kt_sb[0:64, he // 2, c * 128:(c + 1) * 128],
                        q_sb[0:64, he // 2, c * 128:(c + 1) * 128],
                        start=True, stop=True, tile_position=(0, 0)),
                        waits if first else (), None)
                    first = False
                    m = b.op("tensor", lambda e, ho=ho, col=col, c=c, sO=sO: e.matmul(
                        sO[:, col:col + 128], kt_sb[64:128, ho // 2, c * 128:(c + 1) * 128],
                        q_sb[64:128, ho // 2, c * 128:(c + 1) * 128],
                        start=True, stop=True, tile_position=(64, 0)),
                        (), sp if jj == 4 * hw + 3 else None)
                mask_b = bass.AP(mask_sb, 0, [[128, 128], [0, 4], [1, 128]])
                ve = b.op("vector", lambda e, hw=hw, mask_b=mask_b, sE=sE, cb=cb: e.tensor_tensor(
                    sc_sb[:, 8 * hw:8 * hw + 8:2, cb:cb + 128],
                    sE[:, :].rearrange("p (j t) -> p j t", t=128), mask_b, op=AL.mult),
                    [(sp, m), war], sv)
                vo = b.op("vector", lambda e, hw=hw, mask_b=mask_b, sO=sO, cb=cb: e.tensor_tensor(
                    sc_sb[:, 8 * hw + 1:8 * hw + 8:2, cb:cb + 128],
                    sO[:, :].rearrange("p (j t) -> p j t", t=128), mask_b, op=AL.mult),
                    [(sv, ve)], sv)
                last_sc_evac[hw] = vo
                hw_masks.append(vo)
            return hw_masks, cb

        def emit_consumers(c, hw_masks, cb):
            gc = gchunk[0]
            for jj in range(8):
                he, ho = 2 * jj, 2 * jj + 1
                if jj == 0:
                    w0 = [(sv, hw_masks[0]), (ss, last_y_evac[0]), (ss, kv_v)]
                elif jj == 4:
                    w0 = [(sv, hw_masks[1])]
                else:
                    w0 = []
                stp = (gc == 0) and (jj in (3, 7))
                yp = y_psA if jj < 4 else y_psB
                col = (jj % 4) * 128
                b.op("tensor", lambda e, he=he, yp=yp, col=col, jj=jj, c=c, stp=stp, cb=cb: e.matmul(
                    yp[0:64, col:col + 128], v_sb[:, c, he * 64:(he + 1) * 64],
                    sc_sb[:, he, cb:cb + 128], start=(jj in (0, 4)), stop=stp,
                    tile_position=(0, 0)), w0, None)
                b.op("tensor", lambda e, ho=ho, yp=yp, col=col, jj=jj, c=c, stp=stp, cb=cb: e.matmul(
                    yp[64:128, col:col + 128], v_sb[:, c, ho * 64:(ho + 1) * 64],
                    sc_sb[:, ho, cb:cb + 128], start=(jj in (0, 4)), stop=stp,
                    tile_position=(0, 64), skip_group_check=True), (), None)
            if gc > 0:
                for jj in range(8):
                    he, ho = 2 * jj, 2 * jj + 1
                    w0 = [last_stcopy[0]] if jj == 0 else ()
                    stp = jj in (3, 7)
                    yp = y_psA if jj < 4 else y_psB
                    col = (jj % 4) * 128
                    b.op("tensor", lambda e, jj=jj, he=he, yp=yp, col=col, c=c, stp=stp: e.matmul(
                        yp[0:64, col:col + 128], st_sb[0:64, jj, :],
                        q_sb[0:64, he // 2, c * 128:(c + 1) * 128], start=False, stop=stp,
                        tile_position=(0, 0)), w0, None)
                    b.op("tensor", lambda e, jj=jj, ho=ho, yp=yp, col=col, c=c, stp=stp: e.matmul(
                        yp[64:128, col:col + 128], st_sb[64:128, jj, :],
                        q_sb[64:128, ho // 2, c * 128:(c + 1) * 128], start=False, stop=stp,
                        tile_position=(64, 64), skip_group_check=True), (), None)
            for jj in range(8):
                he, ho = 2 * jj, 2 * jj + 1
                w0 = [(sv, last_stcopy_v[0]), (ss, kv_v), (sv, knv)] if jj == 0 else ()
                b.op("tensor", lambda e, he=he, jj=jj, c=c, st=(jj == 0), fl=(jj == 7): e.matmul(
                    delta_ps[0:64, jj * 64:(jj + 1) * 64], kn_sb[:, c, he * 64:(he + 1) * 64],
                    v_sb[:, c, he * 64:(he + 1) * 64], start=st, stop=fl,
                    tile_position=(0, 0)), w0, None)
                m = b.op("tensor", lambda e, ho=ho, jj=jj, c=c, st=(jj == 0), fl=(jj == 7): e.matmul(
                    delta_ps[64:128, jj * 64:(jj + 1) * 64], kn_sb[:, c, ho * 64:(ho + 1) * 64],
                    v_sb[:, c, ho * 64:(ho + 1) * 64], start=st, stop=fl,
                    tile_position=(0, 64), skip_group_check=True), (), sp if jj == 7 else None)
            last_upd[0] = m
            last_inter[0] = m
            upd_m_g[gc] = m
            fold = b.op("vector", lambda e: e.scalar_tensor_tensor(
                stateF[:, :], delta_ps[:, :], 1.0, stateF[:, :],
                op0=AL.mult, op1=AL.add), [(sp, m)], sv)
            last_stcopy_v[0] = fold
            if t == n_tiles_a - 1:
                stc = b.op("vector", lambda e: e.tensor_copy(
                    st_sb[:, :, :], stateF[:, :].rearrange("p (j e) -> p j e", e=64)),
                    [(sv, fold)], sv)
                last_stcopy[0] = (sv, stc)
            else:
                stc = b.op("gpsimd", lambda e: e.tensor_copy(
                    st_sb[:, :, :], stateF[:, :].rearrange("p (j e) -> p j e", e=64)),
                    [(sv, fold)], sg)
                last_stcopy[0] = (sg, stc)
            b.op("scalar", lambda e, c=c: e.copy(
                y_sb[:, 0:4, c * 128:(c + 1) * 128],
                y_psA[:, :].rearrange("p (j t) -> p j t", t=128)), [(sp, m)], ss)
            last_y_evac[0] = b.op("scalar", lambda e, c=c: e.copy(
                y_sb[:, 4:8, c * 128:(c + 1) * 128],
                y_psB[:, :].rearrange("p (j t) -> p j t", t=128)), (), ss)
            y_ev_c[c] = last_y_evac[0]
            gchunk[0] += 1

        sc_cur = emit_scores(0)
        for c in range(NCH):
            if pieces is not None and c in (1, 2):
                (pieces[c - 1])()
            hw_masks, cb = sc_cur
            if c + 1 < NCH:
                sc_cur = emit_scores(c + 1)
            emit_consumers(c, hw_masks, cb)

        rot_state[0]["free"] = (sv, last_sc_evac[1])
        rot_state[1]["free"] = (sv, last_sc_evac[1])

        if pieces is not None:
            (pieces[2])()
            (pieces[3])()
        if t == n_tiles_a - 1:
            gate_clsB_sp[0] = sp.n
            gate_clsB_sv[0] = sv.n
            gate_clsB_ss[0] = ss.n

        base = 72 + 4 * par
        attn_ev = 0
        lastv = 0
        lasts = 0
        sg0 = [sum_guard.get(base, (sv, 0))]
        for s in range(NSA):
            for nn in range(2):
                ri = rot_next()
                waits = [(ss, y_ev_c[s]), (d_wa, 48), rot_state[ri]["free"]]
                for j in range(8):
                    m = b.op("tensor", lambda e, ri=ri, s=s, nn=nn, j=j: e.matmul(
                        rot[ri][:, 0:512], y_sb[:, j, s * 128:(s + 1) * 128],
                        wo_sb[:, j, nn * 512:(nn + 1) * 512], start=(j == 0), stop=(j == 7)),
                        waits if j == 0 else (), sp if j == 7 else None)
                attn_ev = b.op("scalar", lambda e, ri=ri, s=s, nn=nn: e.copy(
                    k_sb[:, s, nn * 512:(nn + 1) * 512], rot[ri][:, 0:512]),
                    [(sp, m), (sv, knv)], ss)
                rot_state[ri]["free"] = (ss, attn_ev)
                g2 = 2 * s + nn
                if pieces is not None and 2 <= g2 <= 5:
                    h1_pend = pieces[4](g2 - 2)
            lastv = b.op("vector", lambda e, s=s, base=base: e.tensor_scalar(
                trash_v[:, :], k_sb[:, s], 1.0, 0.0, op0=AL.mult, op1=AL.add,
                accum_out=sumx[:, base + s:base + s + 1]),
                [(ss, attn_ev), (sv, trash_tok[0])], sv)
            trash_tok[0] = lastv
            lasts = b.op("scalar", lambda e, s=s, base=base: e.activation(
                sq_sb[:, :], k_sb[:, s], AF.Square,
                accum_out=sumq[:, base + s:base + s + 1]),
                [(ss, attn_ev), (ss, sq_tok[0])] + (sg0 if s == 0 else []), ss)
            sq_tok[0] = lasts

        lnb0 = None
        if t == n_tiles_a - 1:
            gate_clsC_sp[0] = sp.n
            gate_clsC_ss[0] = ss.n
            gate_clsC_sv[0] = sv.n
            emit_wu_dmas()
            if n_tiles_b > 0:
                lnb0 = b_ln_pieces(0)
                lnb0[0]()
        b.op("vector", lambda e: None, [(ss, lasts)])
        vch = ln_chain(slice(base, base + NSA), D, lastv, guard=stat_guard.get(base))
        sum_guard[base] = (sv, vch)
        if lnb0 is not None:
            lnb0[1]()
            h2_pend0[0] = lnb0[2]()
        nrm = vch
        residg = 0
        rsem = sv
        reng = "vector"
        for s in range(NSA):
            nrm = b.op("vector", lambda e, s=s, base=base: e.tensor_scalar(
                k_sb[:, s], k_sb[:, s], aa[:, base + s:base + s + 1],
                bb[:, base + s:base + s + 1], op0=AL.mult, op1=AL.add), [(sv, nrm)], sv)
            residg = b.op(reng, lambda e, s=s, par=par: e.tensor_tensor(
                x_sb[:, par, s], x_sb[:, par, s], k_sb[:, s], op=AL.add), [(sv, nrm)], rsem)
        stat_guard[base] = (rsem, residg)
        attn_free[0] = (rsem, residg)
        b.op("sync", lambda e, t=t, par=par: e.dma_start(
            x2scr[t * TA:(t + 1) * TA, :].rearrange("(s p) c -> p s c", p=128),
            x_sb[:, par]), [(rsem, residg), (d_x2w, 16 * t)], d_x2w, 16)

    spA, svA, ssA, sgA = sp.n, sv.n, ss.n, sg.n

    pctxA.close()
    actxA.close()
    bctx = ExitStack()

    def sbb(name, shape, dt):
        return bctx.enter_context(nc.sbuf_tensor(name, shape, dt))

    wg_sb = sbb("wg_sb", [128, 8, HID], BF16)
    wtens["wg"] = wg_sb
    x2_sb = sbb("x2_sb", [128, 2, NSB, 1024], F32)
    wtens["x2"] = x2_sb
    h2_sb = sbb("h2_sb", [128, 2, NSB, 1024], BF16)
    wtens["h2"] = h2_sb
    h2t_sb = sbb("h2t_sb", [128, 2, 8, TB], BF16)
    wu_sb = sbb("wu_sb", [128, 8, HID], BF16)
    wtens["wu"] = wu_sb
    sg_sb = sbb("sg_sb", [128, 24, TB], BF16)
    f_sb = sbb("f_sb", [128, NSB, 1024], F32)
    wd_sb = sbb("wd_sb", [128, 24, 1024], BF16)

    pb = [bctx.enter_context(nc.psum_tensor(f"pb{i}", [128, 512], F32)) for i in range(8)]

    a_done = [(sp, spA), (sv, svA), (ss, ssA), (sg, sgA), (d_x2w, 16 * n_tiles_a)]
    b.op("gpsimd", lambda e: e.dma_start(
        wd_sb[:, :, 0:512], wdt[:, 0:512].rearrange("(j p) c -> p j c", p=128)),
        a_done + [(d_wb, 16 * wdma_n[0])], d_wb, 16)
    wd_rdy = 16 * (wdma_n[0] + 1)
    wdma_n[0] += 1
    b.op("gpsimd", lambda e: e.dma_start(
        wd_sb[:, :, 512:1024], wdt[:, 512:1024].rearrange("(j p) c -> p j c", p=128)),
        [(d_wb, 16 * wdma_n[0])], d_wb, 16)
    wd_rdy2 = 16 * (wdma_n[0] + 1)
    wdma_n[0] += 1

    stt_tok = {}
    bank_state = [{"free": (sv, 0), "free_s": (ss, 0)} for _ in range(4)] + \
        [{"free": (sv, gate_clsC_sv[0]), "free_s": (ss, gate_clsC_ss[0])} for _ in range(4)]
    blocksB = [(s, j) for s in range(NSB) for j in range(8)]

    def b_transpose(tt, h2_last):
        par = tt % 2
        ht_ev = 0
        for g in range(0, 16, 4):
            grp = blocksB[g:g + 4]
            ri = (g // 4) % 4
            waits = [(sv, h2_last), (d_ms, 32), bank_state[ri]["free"], bank_state[ri]["free_s"]]
            if tt == 0 and g == 0:
                waits.append((sp, gate_clsC_sp[0]))
                waits.append((ss, gate_clsC_ss[0]))
            for gi, (s, j) in enumerate(grp):
                m = b.op("tensor", lambda e, ri=ri, gi=gi, s=s, j=j, par=par: e.matmul(
                    pb[ri][:, gi * 128:(gi + 1) * 128], h2_sb[:, par, s, j * 128:(j + 1) * 128],
                    ident_sb[:, :], start=True, stop=True),
                    waits if gi == 0 else (), sp if gi == 3 else None)
            s0, j0 = grp[0]
            ev = b.op("vector", lambda e, ri=ri, s0=s0, j0=j0, par=par: e.tensor_copy(
                h2t_sb[:, par, j0:j0 + 4, s0 * 128:(s0 + 1) * 128],
                pb[ri][:, :].rearrange("p (g c) -> p g c", c=128)), [(sp, m)], sv)
            bank_state[ri]["free"] = (sv, ev)
            ht_ev = ev
        trans_m[par] = sp.n
        return ht_ev

    def make_fln(t, f_ev_s):
        par = t % 2
        base = 88 + 4 * par
        st = {}
        last = (t == n_tiles_b - 1)
        reng_b, rsem_b = ("vector", sv) if last else ("gpsimd", sg)

        if last:
            def p_last():
                sg0 = [sum_guard.get(base, (sv, 0))]
                lv = 0
                lq = 0
                for s in range(NSB):
                    lv = b.op("vector", lambda e, s=s: e.tensor_scalar(
                        trash_v[:, :], f_sb[:, s], 1.0, 0.0, op0=AL.mult, op1=AL.add,
                        accum_out=sumx[:, base + s:base + s + 1]),
                        [(ss, f_ev_s[s]), (sv, trash_tok[0])], sv)
                    trash_tok[0] = lv
                    lq = b.op("scalar", lambda e, s=s: e.activation(
                        sq_sb[:, :], f_sb[:, s], AF.Square,
                        accum_out=sumq[:, base + s:base + s + 1]),
                        [(ss, f_ev_s[s]), (ss, sq_tok[0])] + (sg0 if s == 0 else []), ss)
                    sq_tok[0] = lq
                b.op("vector", lambda e: None, [(ss, lq)])
                vch = ln_chain(slice(base, base + NSB), D, lv)
                for s in range(NSB):
                    nrm = b.op("vector", lambda e, s=s: e.tensor_scalar(
                        f_sb[:, s], f_sb[:, s], aa[:, base + s:base + s + 1],
                        bb[:, base + s:base + s + 1], op0=AL.mult, op1=AL.add),
                        [(sv, vch)], sv)
                    residg = b.op("vector", lambda e, s=s: e.tensor_tensor(
                        f_sb[:, s], x2_sb[:, par, s], f_sb[:, s], op=AL.add),
                        [(sv, nrm)], sv)
                    b.op("sync", lambda e, s=s: e.dma_start(
                        out_d[t * TB + s * 128:t * TB + (s + 1) * 128, :].rearrange(
                            "(s p) c -> p s c", p=128), f_sb[:, s:s + 1]),
                        [(sv, residg), (d_out, 16 * (t + s))], d_out, 16)

            def p_nop():
                pass
            return [p_nop, p_nop, p_last]

        def p0():
            lastv = 0
            for s in range(NSB):
                lastv = b.op("vector", lambda e, s=s: e.tensor_scalar(
                    trash_v[:, :], f_sb[:, s], 1.0, 0.0, op0=AL.mult, op1=AL.add,
                    accum_out=sumx[:, base + s:base + s + 1]),
                    [(ss, f_ev_s[s]), (sv, trash_tok[0])], sv)
                trash_tok[0] = lastv
            lasts = 0
            sg0 = [sum_guard.get(base, (sv, 0))]
            for s in range(NSB):
                lasts = b.op("scalar", lambda e, s=s: e.activation(
                    sq_sb[:, :], f_sb[:, s], AF.Square,
                    accum_out=sumq[:, base + s:base + s + 1]),
                    [(ss, f_ev_s[s]), (ss, sq_tok[0])] + (sg0 if s == 0 else []), ss)
                sq_tok[0] = lasts
            st["lastv"], st["lasts"] = lastv, lasts

        def p1():
            b.op("vector", lambda e: None, [(ss, st["lasts"])])
            vch = ln_chain(slice(base, base + NSB), D, st["lastv"])
            sum_guard[base] = (sv, vch)
            st["vch"] = vch

        def p2():
            nrm = st["vch"]
            residg = 0
            for s in range(NSB):
                nrm = b.op("vector", lambda e, s=s: e.tensor_scalar(
                    f_sb[:, s], f_sb[:, s], aa[:, base + s:base + s + 1],
                    bb[:, base + s:base + s + 1], op0=AL.mult, op1=AL.add), [(sv, nrm)], sv)
                residg = b.op(reng_b, lambda e, s=s: e.tensor_tensor(
                    f_sb[:, s], x2_sb[:, par, s], f_sb[:, s], op=AL.add), [(sv, nrm)], rsem_b)
            resid_done[0] = residg
            b.op("sync", lambda e: e.dma_start(
                out_d[t * TB:(t + 1) * TB, :].rearrange("(s p) c -> p s c", p=128),
                f_sb[:, :]), [(rsem_b, residg), (d_out, 16 * t)], d_out, 16)
        return [p0, p1, p2]

    if n_tiles_b > 0:
        ht_pend = b_transpose(0, h2_pend0[0])

    fln_pend = [None]
    for t in range(n_tiles_b):
        par = t % 2
        lnb = b_ln_pieces(t + 1) if t + 1 < n_tiles_b else None
        fp = fln_pend[0]

        h_ready = 0
        if t == 0:
            e1_tok = {}
            for mi in range(24):
                rg = mi % 8
                gw = [(sv, ht_pend), (d_wb, wg_rdy[mi // 6]),
                      bank_state[rg]["free"], bank_state[rg]["free_s"]]
                for k in range(8):
                    mg = b.op("tensor", lambda e, rg=rg, mi=mi, k=k, par=par: e.matmul(
                        pb[rg][:, 0:TB], wg_sb[:, k, mi * 128:(mi + 1) * 128],
                        h2t_sb[:, par, k, :], start=(k == 0), stop=(k == 7)),
                        gw if k == 0 else (), sp if k == 7 else None)
                e1 = b.op("scalar", lambda e, rg=rg, mi=mi: e.activation(
                    sg_sb[:, mi, :], pb[rg][:, 0:TB], AF.Silu, scale=SC_FF),
                    [(sp, mg)] + (a_done if mi == 0 else []), ss)
                bank_state[rg]["free_s"] = (ss, e1)
                e1_tok[mi] = e1
                if lnb is not None:
                    if mi == 13:
                        lnb[0]()
                    elif mi == 15:
                        lnb[1]()
                    elif mi == 17:
                        h2_next = lnb[2]()
            for mi in range(24):
                ru = mi % 8
                uw = [(d_wb, wu_rdy[mi // 12]),
                      bank_state[ru]["free"], bank_state[ru]["free_s"]]
                for k in range(8):
                    mu = b.op("tensor", lambda e, ru=ru, mi=mi, k=k, par=par: e.matmul(
                        pb[ru][:, 0:TB], wu_sb[:, k, mi * 128:(mi + 1) * 128],
                        h2t_sb[:, par, k, :], start=(k == 0), stop=(k == 7)),
                        uw if k == 0 else (), sp if k == 7 else None)
                e2 = b.op("vector", lambda e, ru=ru, mi=mi: e.scalar_tensor_tensor(
                    sg_sb[:, mi, :], pb[ru][:, 0:TB], 1.0, sg_sb[:, mi, :],
                    op0=AL.mult, op1=AL.mult), [(sp, mu), (ss, e1_tok[mi])], sv)
                bank_state[ru]["free"] = (sv, e2)
                stt_tok[mi] = e2
                h_ready = e2
        else:
            for mi in range(24):
                rg = (2 * mi) % 8
                ru = (2 * mi + 1) % 8
                gw = [(sv, ht_pend), (d_wb, wg_rdy[mi // 6]), bank_state[rg]["free"], bank_state[rg]["free_s"]]
                for k in range(8):
                    mg = b.op("tensor", lambda e, rg=rg, mi=mi, k=k, par=par: e.matmul(
                        pb[rg][:, 0:TB], wg_sb[:, k, mi * 128:(mi + 1) * 128],
                        h2t_sb[:, par, k, :], start=(k == 0), stop=(k == 7)),
                        gw if k == 0 else (), sp if k == 7 else None)
                uw = [(d_wb, wu_rdy[mi // 12]), bank_state[ru]["free"], bank_state[ru]["free_s"]]
                for k in range(8):
                    mu = b.op("tensor", lambda e, ru=ru, mi=mi, k=k, par=par: e.matmul(
                        pb[ru][:, 0:TB], wu_sb[:, k, mi * 128:(mi + 1) * 128],
                        h2t_sb[:, par, k, :], start=(k == 0), stop=(k == 7)),
                        uw if k == 0 else (), sp if k == 7 else None)
                e1 = b.op("scalar", lambda e, rg=rg, mi=mi: e.activation(
                    sg_sb[:, mi, :], pb[rg][:, 0:TB], AF.Silu, scale=SC_FF),
                    [(sp, mg)], ss)
                bank_state[rg]["free_s"] = (ss, e1)
                e2 = b.op("vector", lambda e, ru=ru, mi=mi: e.scalar_tensor_tensor(
                    sg_sb[:, mi, :], pb[ru][:, 0:TB], 1.0, sg_sb[:, mi, :],
                    op0=AL.mult, op1=AL.mult), [(sp, mu), (ss, e1)], sv)
                bank_state[ru]["free"] = (sv, e2)
                bank_state[rg]["free"] = (sv, e2)
                stt_tok[mi] = e2
                h_ready = e2
                if fp is not None:
                    if mi == 1:
                        fp[0]()
                    elif mi == 3:
                        fp[1]()
                    elif mi == 5:
                        fp[2]()
                if lnb is not None:
                    if mi == 13:
                        lnb[0]()
                    elif mi == 15:
                        lnb[1]()
                    elif mi == 17:
                        h2_next = lnb[2]()

        if lnb is not None:
            ht_pend = b_transpose(t + 1, h2_next)

        f_ev = 0
        f_ev_s = [0] * NSB
        for s in range(NSB):
            for nn in range(2):
                ri = 4 + 2 * s + nn
                wdw = (d_wb, wd_rdy) if nn == 0 else (d_wb, wd_rdy2)
                waits = [wdw, bank_state[ri]["free"], bank_state[ri]["free_s"]]
                for kh in range(24):
                    m = b.op("tensor", lambda e, ri=ri, s=s, nn=nn, kh=kh: e.matmul(
                        pb[ri][:, 0:512], sg_sb[:, kh, s * 128:(s + 1) * 128],
                        wd_sb[:, kh, nn * 512:(nn + 1) * 512], start=(kh == 0), stop=(kh == 23)),
                        (waits if kh == 0 else []) + [(sv, stt_tok[kh])], sp if kh == 23 else None)
                f_ev = b.op("scalar", lambda e, ri=ri, s=s, nn=nn: e.activation(
                    f_sb[:, s, nn * 512:(nn + 1) * 512], pb[ri][:, 0:512],
                    AF.Identity, scale=SC_DN), [(sp, m), (d_out, 16 * t)], ss)
                bank_state[ri]["free_s"] = (ss, f_ev)
            f_ev_s[s] = f_ev

        fln_pend[0] = make_fln(t, f_ev_s)

    if fln_pend[0] is not None:
        fp = fln_pend[0]
        fp[0]()
        fp[1]()
        fp[2]()

    finals = [(d_out, 16 * (n_tiles_b + 1)), (d_x2r, 16 * n_tiles_b), (d_x2w, 16 * n_tiles_a),
              (d_x, 16 * (n_tiles_a - 1)), (d_x0, 16 * NSA), (d_wa, 48), (d_wb, 128),
              (d_ms, 32)]
    b.op("sync", lambda e: None, [(s, v) for s, v in finals if v > 0])

    emit(nc, b, ctx)
    bctx.close()
    ctx.close()
    return nc


_mask_np = np.triu(np.ones((128, 128), dtype=np.float32)).astype(ml_dtypes.bfloat16)
_ident_np = np.eye(128, dtype=np.float32).astype(ml_dtypes.bfloat16)


def kernel(L0_attn_x, L0_attn_W_qkv, L0_attn_W_o, L0_ffn_w_gate, L0_ffn_w_up, L0_ffn_w_down):
    x = np.ascontiguousarray(np.asarray(L0_attn_x, dtype=np.float32))
    wqkv = np.ascontiguousarray(np.asarray(L0_attn_W_qkv, dtype=np.float32))
    wo = np.ascontiguousarray(np.asarray(L0_attn_W_o, dtype=np.float32))
    wgt = np.ascontiguousarray(np.asarray(L0_ffn_w_gate, dtype=np.int32).T)
    wut = np.ascontiguousarray(np.asarray(L0_ffn_w_up, dtype=np.int32).T)
    wdt = np.ascontiguousarray(np.asarray(L0_ffn_w_down, dtype=np.int32).T)

    nc = build()
    in_maps = []
    for bt in range(8):
        in_maps.append({
            "x_in": x[bt], "wqkv": wqkv, "wo": wo,
            "wgt": wgt, "wut": wut, "wdt": wdt,
            "maskd": _mask_np, "identd": _ident_np,
        })
    res = run_bass_kernel_spmd(nc, in_maps, core_ids=list(range(8)))
    return np.stack([res.results[i]["out"] for i in range(8)], axis=0)


# version 35
# speedup vs baseline: 1.0395x; 1.0301x over previous
import numpy as np
import ml_dtypes
from contextlib import ExitStack

import concourse.bass as bass
import concourse.mybir as mybir
from concourse.bass_utils import run_bass_kernel_spmd

F32 = mybir.dt.float32
BF16 = mybir.dt.bfloat16
I32 = mybir.dt.int32
AL = mybir.AluOpType
AF = mybir.ActivationFunctionType
AX = mybir.AxisListType

D = 1024
H = 16
DK = 64
L = 4096
HID = 3072
EPS = 1e-5
CH = 128
TA = 512
TB = 256
NTA = L // TA
NTB = L // TB
NSA = TA // 128
NSB = TB // 128
NCH = TA // CH
MAGIC = 0x5F3759DF
SC_FF = 1.0 / 127.0
SC_DN = 1.0 / (127.0 * 127.0)

ENGINES = ("sync", "tensor", "vector", "scalar", "gpsimd")


class Sem:
    __slots__ = ("name", "n", "h")

    def __init__(self, name):
        self.name = name
        self.n = 0
        self.h = None


class B:

    FAST = True

    def __init__(self):
        self.ops = {e: [] for e in ENGINES}
        self.sems = []
        self.auto = {}

    def sem(self, name):
        s = Sem(name)
        self.sems.append(s)
        return s

    def op(self, eng, fn, waits=(), inc=None, amt=1):
        w = [(s, v) for (s, v) in waits if v > 0]
        a = self.auto.get(eng)
        if a is not None and a.n > 0 and not B.FAST:
            w.append((a, a.n))
        self.ops[eng].append((w, fn, inc, amt))
        if inc is not None:
            inc.n += amt
            return inc.n
        return None


def emit(nc, b, ctx):
    for s in b.sems:
        s.h = ctx.enter_context(nc.semaphore(s.name))
    final = {id(s): s.n for s in b.sems}
    for e in ENGINES:
        for w, _, _, _ in b.ops[e]:
            for s, v in w:
                assert v <= final[id(s)], \
                    f"wait {s.name}>={v} unreachable (final {final[id(s)]})"
    block = ctx.enter_context(nc.Block())

    def run(eng, ops):
        seen = {}
        for w, fn, inc, amt in ops:
            for s, v in w:
                if seen.get(id(s), -1) < v:
                    eng.wait_ge(s.h, v)
                    seen[id(s)] = v
            ins = fn(eng)
            if inc is not None:
                assert ins is not None
                ins.then_inc(inc.h, amt)

    block.sync(lambda e: run(e, b.ops["sync"]))
    block.tensor(lambda e: run(e, b.ops["tensor"]))
    block.vector(lambda e: run(e, b.ops["vector"]))
    block.scalar(lambda e: run(e, b.ops["scalar"]))
    block.gpsimd(lambda e: run(e, b.ops["gpsimd"]))


def build(n_tiles_a=NTA, n_tiles_b=NTB):
    nc = bass.Bass("TRN2", target_bir_lowering=False)

    x_in = nc.dram_tensor("x_in", [L, D], F32, kind="ExternalInput").ap()
    wqkv = nc.dram_tensor("wqkv", [D, 3 * D], F32, kind="ExternalInput").ap()
    wo = nc.dram_tensor("wo", [D, D], F32, kind="ExternalInput").ap()
    wgt = nc.dram_tensor("wgt", [D, HID], I32, kind="ExternalInput").ap()
    wut = nc.dram_tensor("wut", [D, HID], I32, kind="ExternalInput").ap()
    wdt = nc.dram_tensor("wdt", [HID, D], I32, kind="ExternalInput").ap()
    maskd = nc.dram_tensor("maskd", [128, 128], BF16, kind="ExternalInput").ap()
    identd = nc.dram_tensor("identd", [128, 128], BF16, kind="ExternalInput").ap()
    out_d = nc.dram_tensor("out", [L, D], F32, kind="ExternalOutput").ap()
    x2scr = nc.dram_tensor("x2scr", [L, D], F32).ap()

    b = B()
    d_x0 = b.sem("d_x0")
    d_x = b.sem("d_x")
    d_wa = b.sem("d_wa")
    d_wb = b.sem("d_wb")
    d_ms = b.sem("d_ms")
    d_x2w = b.sem("d_x2w")
    d_x2r = b.sem("d_x2r")
    d_out = b.sem("d_out")
    sp = b.sem("sp")
    sv = b.sem("sv")
    ss = b.sem("ss")
    sg = b.sem("sg")
    b.auto = {"vector": sv, "scalar": ss, "gpsimd": sg}

    ctx = ExitStack()

    def sb(name, shape, dt):
        return ctx.enter_context(nc.sbuf_tensor(name, shape, dt))

    mask_sb = sb("mask_sb", [128, 128], BF16)
    ident_sb = sb("ident_sb", [128, 128], BF16)
    magic_sb = sb("magic_sb", [128, 64], I32)
    sumx = sb("sumx", [128, 96], F32)
    sumq = sb("sumq", [128, 96], F32)
    aa = sb("aa", [128, 96], F32)
    bb = sb("bb", [128, 96], F32)
    c1 = sb("c1", [128, 64], F32)
    c2 = sb("c2", [128, 64], F32)
    c4 = sb("c4", [128, 64], F32)
    ish = sb("ish", [128, 64], I32)
    trash_v = sb("trash_v", [128, 1024], F32)
    sq_sb = sb("sq_sb", [128, 1024], F32)

    aa_guard = [(ss, 0)]
    ln_v = [0]

    def ln_chain(sl, N, last_v, guard=None):
        def w(v):
            return [(sv, v)]
        z = slice(0, sl.stop - sl.start)
        v = last_v
        v = b.op("vector", lambda e, s=sl, z=z: e.tensor_scalar_mul(c1[:, z], sumx[:, s], 1.0 / N),
                 w(v) + [guard if guard is not None else aa_guard[0]], sv)
        v = b.op("vector", lambda e, z=z: e.tensor_mul(c4[:, z], c1[:, z], c1[:, z]), w(v), sv)
        v = b.op("vector", lambda e, s=sl, z=z: e.scalar_tensor_tensor(
            c2[:, z], sumq[:, s], 1.0 / N, c4[:, z], op0=AL.mult, op1=AL.subtract), w(v), sv)
        v = b.op("vector", lambda e, z=z: e.tensor_scalar_add(c2[:, z], c2[:, z], EPS), w(v), sv)
        v = b.op("vector", lambda e, z=z: e.tensor_scalar(
            ish[:, z], c2[:, z].bitcast(I32), 1, None, op0=AL.arith_shift_right), w(v), sv)
        v = b.op("vector", lambda e, s=sl, z=z: e.tensor_tensor(
            aa[:, s].bitcast(I32), magic_sb[:, z], ish[:, z], op=AL.subtract), w(v), sv)
        for _ in range(3):
            v = b.op("vector", lambda e, s=sl, z=z: e.tensor_mul(c4[:, z], aa[:, s], aa[:, s]), w(v), sv)
            v = b.op("vector", lambda e, z=z: e.scalar_tensor_tensor(
                c4[:, z], c4[:, z], -0.5, c2[:, z], op0=AL.mult, op1=AL.mult), w(v), sv)
            v = b.op("vector", lambda e, s=sl, z=z: e.scalar_tensor_tensor(
                aa[:, s], c4[:, z], 1.5, aa[:, s], op0=AL.add, op1=AL.mult), w(v), sv)
        v = b.op("vector", lambda e, s=sl, z=z: e.scalar_tensor_tensor(
            bb[:, s], c1[:, z], -1.0, aa[:, s], op0=AL.mult, op1=AL.mult), w(v), sv)
        return v

    def ln_chain_part1(sl, N, last_v, guard=None, swait=None):
        def w(v):
            return [(sv, v)]
        z = slice(0, sl.stop - sl.start)
        w0 = w(last_v) + ([guard] if guard is not None else [])
        if swait is not None:
            w0.append(swait)
        v = b.op("vector", lambda e, s=sl, z=z: e.tensor_scalar_mul(c1[:, z], sumx[:, s], 1.0 / N),
                 w0, sv)
        v = b.op("vector", lambda e, z=z: e.tensor_mul(c4[:, z], c1[:, z], c1[:, z]), w(v), sv)
        v = b.op("vector", lambda e, s=sl, z=z: e.scalar_tensor_tensor(
            c2[:, z], sumq[:, s], 1.0 / N, c4[:, z], op0=AL.mult, op1=AL.subtract), w(v), sv)
        v = b.op("vector", lambda e, z=z: e.tensor_scalar_add(c2[:, z], c2[:, z], EPS), w(v), sv)
        v = b.op("vector", lambda e, z=z: e.tensor_scalar(
            ish[:, z], c2[:, z].bitcast(I32), 1, None, op0=AL.arith_shift_right), w(v), sv)
        v = b.op("vector", lambda e, s=sl, z=z: e.tensor_tensor(
            aa[:, s].bitcast(I32), magic_sb[:, z], ish[:, z], op=AL.subtract), w(v), sv)
        return v

    def ln_chain_part2(sl, last_v):
        def w(v):
            return [(sv, v)]
        z = slice(0, sl.stop - sl.start)
        v = last_v
        for _ in range(3):
            v = b.op("vector", lambda e, s=sl, z=z: e.tensor_mul(c4[:, z], aa[:, s], aa[:, s]), w(v), sv)
            v = b.op("vector", lambda e, z=z: e.scalar_tensor_tensor(
                c4[:, z], c4[:, z], -0.5, c2[:, z], op0=AL.mult, op1=AL.mult), w(v), sv)
            v = b.op("vector", lambda e, s=sl, z=z: e.scalar_tensor_tensor(
                aa[:, s], c4[:, z], 1.5, aa[:, s], op0=AL.add, op1=AL.mult), w(v), sv)
        v = b.op("vector", lambda e, s=sl, z=z: e.scalar_tensor_tensor(
            bb[:, s], c1[:, z], -1.0, aa[:, s], op0=AL.mult, op1=AL.mult), w(v), sv)
        return v

    actxA = ExitStack()

    def sba(name, shape, dt):
        return actxA.enter_context(nc.sbuf_tensor(name, shape, dt))

    wq_sb = sba("wq_sb", [128, 8, 1024], BF16)
    wkv_sb = sba("wkv_sb", [128, 8, 2048], BF16)
    h1_sb = sba("h1_sb", [128, NSA, 1024], BF16)
    h1t_sb = sba("h1t_sb", [128, 8, TA], BF16)
    ksq_sb = sba("ksq_sb", [128, 1024], F32)
    ksq2_sb = sba("ksq2_sb", [128, 1024], F32)
    q_sb = sba("q_sb", [128, 8, TA], BF16)
    kn_sb = sba("kn_sb", [128, NSA, 1024], BF16)
    kt_sb = sba("kt_sb", [128, 8, TA], BF16)
    v_sb = sba("v_sb", [128, NSA, 1024], BF16)
    sc_sb = sba("sc_sb", [128, 16, 256], BF16)
    scr_sb = sba("scr_sb", [128, 2, 512], BF16)
    st_sb = sba("st_sb", [128, 8, 64], BF16)
    stateF = sba("stateF", [128, 512], F32)
    y_sb = sba("y_sb", [128, 8, TA], BF16)
    wo_sb = sba("wo_sb", [128, 8, 1024], BF16)
    k_sb = sba("k_sb", [128, NSA, 1024], F32)
    x_sb = sba("x_sb", [128, 2, NSA, 1024], F32)

    pctxA = ExitStack()

    def psa(name, shape):
        return pctxA.enter_context(nc.psum_tensor(name, shape, F32))

    delta_ps = psa("delta_ps", [128, 512])
    y_psA = psa("y_psA", [128, 512])
    y_psB = psa("y_psB", [128, 512])
    scE_ps = psa("scE_ps", [128, 512])
    scO_ps = psa("scO_ps", [128, 512])
    rot = [psa(f"rot{i}", [128, 512]) for i in range(3)]

    b.op("vector", lambda e: e.memset(magic_sb[:, :], MAGIC), (), sv)
    b.op("vector", lambda e: e.memset(stateF[:, :], 0.0), (), sv)
    for s0_ in range(NSA):
        b.op("sync", lambda e, s=s0_: e.dma_start(
            x_sb[:, 0, s:s + 1], x_in[s * 128:(s + 1) * 128, :].rearrange(
                "(s p) c -> p s c", p=128)), (), d_x0, 16)
    b.op("scalar", lambda e: e.dma_start(mask_sb[:, :], maskd), (), d_ms, 16)
    b.op("scalar", lambda e: e.dma_start(ident_sb[:, :], identd), [(d_ms, 16)], d_ms, 16)
    b.op("scalar", lambda e: e.preload_act_table(AF.Silu), (), None)
    b.op("gpsimd", lambda e: e.dma_start(
        wq_sb[:, :, :], wqkv[:, 0:1024].rearrange("(j p) c -> p j c", p=128)), (), d_wa, 16)
    b.op("gpsimd", lambda e: e.dma_start(
        wkv_sb[:, :, 0:1024], wqkv[:, 1024:2048].rearrange("(j p) c -> p j c", p=128)),
        [(d_wa, 16)], d_wa, 16)
    b.op("gpsimd", lambda e: e.dma_start(
        wkv_sb[:, :, 1024:2048], wqkv[:, 2048:3072].rearrange("(j p) c -> p j c", p=128)),
        [(d_wa, 32)], d_wa, 16)
    b.op("gpsimd", lambda e: e.dma_start(
        wo_sb[:, :, :], wo.rearrange("(j p) c -> p j c", p=128)), [(d_wa, 48)], d_wa, 16)

    rot_state = [{"free": (sv, 0)} for _ in range(3)]
    rot_i = [0]

    def rot_next():
        i = rot_i[0] % 3
        rot_i[0] += 1
        return i

    gate_clsA_sp = [0]
    gate_clsA_sv = [0]
    gate_clsA_ss = [0]
    wdma_n = [0]
    wg_rdy = {}
    wu_rdy = {}
    wtens = {}

    def emit_wg_dmas():
        for ci in range(4):
            w0 = ([(sp, gate_clsA_sp[0]), (sv, gate_clsA_sv[0]),
                   (ss, gate_clsA_ss[0]), (d_wa, 64)]
                  if wdma_n[0] == 0 else [(d_wb, 16 * wdma_n[0])])
            b.op("gpsimd", lambda e, ci=ci: e.dma_start(
                wtens["wg"][:, :, ci * 768:(ci + 1) * 768],
                wgt[:, ci * 768:(ci + 1) * 768].rearrange("(j p) c -> p j c", p=128)),
                w0, d_wb, 16)
            wg_rdy[ci] = 16 * (wdma_n[0] + 1)
            wdma_n[0] += 1

    def emit_wu_dmas():
        for ci in range(2):
            w0 = [(sp, gate_clsC_sp[0]), (ss, gate_clsC_ss[0]),
                  (sv, gate_clsC_sv[0]), last_stcopy[0],
                  (d_wb, 16 * wdma_n[0])]
            b.op("gpsimd", lambda e, ci=ci: e.dma_start(
                wtens["wu"][:, :, ci * 1536:(ci + 1) * 1536],
                wut[:, ci * 1536:(ci + 1) * 1536].rearrange("(j p) c -> p j c", p=128)),
                w0, d_wb, 16)
            wu_rdy[ci] = 16 * (wdma_n[0] + 1)
            wdma_n[0] += 1
    gate_clsB_sp = [0]
    gate_clsB_sv = [0]
    gate_clsB_ss = [0]
    gate_clsC_sp = [0]
    gate_clsC_ss = [0]
    gate_clsC_sv = [0]
    last_upd = [0]
    gscore = [0]
    upd_m_g = {}
    trash_tok = [0]
    sq_tok = [0]
    last_sc_evac = [0, 0]
    last_mask_v = [0]
    last_y_evac = [0]
    y_ev_c = [0, 0, 0, 0]
    last_inter = [0]
    last_stcopy = [(sg, 0)]
    last_stcopy_v = [0]
    h1_read_m = {}
    attn_free = [(sv, 0)]
    stat_guard = {}
    sum_guard = {}
    gchunk = [0]
    blocks = [(s, j) for s in range(NSA) for j in range(8)]

    def a_ln_pieces(tt):
        par = tt % 2
        base = 64 + 4 * par
        st_ = {}

        def xw(s):
            return (d_x0, 16 * (s + 1)) if tt == 0 else (d_x, 16 * tt)

        def p0a():
            lastv = 0
            for s in (0, 1):
                lastv = b.op("vector", lambda e, s=s, par=par, base=base: e.tensor_scalar(
                    trash_v[:, :], x_sb[:, par, s], 1.0, 0.0, op0=AL.mult, op1=AL.add,
                    accum_out=sumx[:, base + s:base + s + 1]),
                    [xw(s), sum_guard.get(base, (sv, 0)), (sv, trash_tok[0])], sv)
                trash_tok[0] = lastv

        def p0b():
            lastv = 0
            for s in (2, 3):
                lastv = b.op("vector", lambda e, s=s, par=par, base=base: e.tensor_scalar(
                    trash_v[:, :], x_sb[:, par, s], 1.0, 0.0, op0=AL.mult, op1=AL.add,
                    accum_out=sumx[:, base + s:base + s + 1]),
                    [xw(s), sum_guard.get(base, (sv, 0)), (sv, trash_tok[0])], sv)
                trash_tok[0] = lastv
            lasts = 0
            for s in range(NSA):
                lasts = b.op("scalar", lambda e, s=s, par=par, base=base: e.activation(
                    sq_sb[:, :], x_sb[:, par, s], AF.Square,
                    accum_out=sumq[:, base + s:base + s + 1]),
                    [xw(s), sum_guard.get(base, (sv, 0)), (ss, sq_tok[0])], ss)
                sq_tok[0] = lasts
            st_["v"] = lastv
            st_["s"] = lasts

        def p1():
            st_["c1"] = ln_chain_part1(slice(base, base + NSA), D, st_["v"],
                                       guard=stat_guard.get(base), swait=(ss, st_["s"]))

        def p2a():
            vch = ln_chain_part2(slice(base, base + NSA), st_["c1"])
            sum_guard[base] = (sv, vch)
            st_["vch"] = vch

        def p2b(s):
            h1_last = b.op("scalar", lambda e, s=s, par=par, base=base: e.activation(
                h1_sb[:, s], x_sb[:, par, s], AF.Identity,
                bias=bb[:, base + s:base + s + 1], scale=aa[:, base + s:base + s + 1]),
                [(sv, st_["vch"]), (sp, h1_read_m.get(0, 0))], ss)
            if s == NSA - 1:
                stat_guard[base] = (ss, h1_last)
            return h1_last
        return p0a, p0b, p1, p2a, p2b

    def a_ln(tt):
        p0a, p0b, p1, p2a, p2b = a_ln_pieces(tt)
        p0a(); p0b(); p1(); p2a()
        h1_last = 0
        h1_s = []
        for s in range(NSA):
            h1_last = p2b(s)
            h1_s.append(h1_last)
        return h1_last, h1_s

    trans_m = {}
    resid_done = [0]
    h2_pend0 = [0]

    def b_ln_pieces(tt):
        parB = tt % 2
        baseB = 80 + 4 * parB
        stB = {}

        def p0():
            if tt >= 2:
                waits = [(d_x2w, 16 * (tt * TB // TA + 1)), (d_x2r, 16 * tt),
                         (sg, resid_done[0])]
                b.op("sync", lambda e: e.dma_start(
                    wtens["x2"][:, parB], x2scr[tt * TB:(tt + 1) * TB, :].rearrange(
                        "(s p) c -> p s c", p=128)), waits, d_x2r, 16)
            x2r = 16 * (tt + 1)
            lastv = 0
            for s in range(NSB):
                lastv = b.op("vector", lambda e, s=s: e.tensor_scalar(
                    trash_v[:, :], wtens["x2"][:, parB, s], 1.0, 0.0, op0=AL.mult, op1=AL.add,
                    accum_out=sumx[:, baseB + s:baseB + s + 1]),
                    [(d_x2r, x2r), (sv, trash_tok[0])], sv)
                trash_tok[0] = lastv
            lasts = 0
            sg0 = [sum_guard.get(baseB, (sv, 0))]
            for s in range(NSB):
                lasts = b.op("scalar", lambda e, s=s: e.activation(
                    sq_sb[:, :], wtens["x2"][:, parB, s], AF.Square,
                    accum_out=sumq[:, baseB + s:baseB + s + 1]),
                    [(d_x2r, x2r), (ss, sq_tok[0])] + (sg0 if s == 0 else []), ss)
                sq_tok[0] = lasts
            stB["lastv"], stB["lasts"] = lastv, lasts

        def p1():
            b.op("vector", lambda e: None, [(ss, stB["lasts"])])
            vch = ln_chain(slice(baseB, baseB + NSB), D, stB["lastv"])
            sum_guard[baseB] = (sv, vch)
            stB["vch"] = vch

        def p2():
            h2_last = 0
            for s in range(NSB):
                h2_last = b.op("vector", lambda e, s=s: e.tensor_scalar(
                    wtens["h2"][:, parB, s], wtens["x2"][:, parB, s],
                    aa[:, baseB + s:baseB + s + 1],
                    bb[:, baseB + s:baseB + s + 1], op0=AL.mult, op1=AL.add),
                    [(sv, stB["vch"]), (sp, trans_m.get(parB, 0))] +
                    ([(sp, gate_clsB_sp[0])] if tt == 0 else []), sv)
            return h2_last
        return [p0, p1, p2]

    h1_pend, h1_pend_s = (a_ln(0) if n_tiles_a > 0 else (0, [0] * NSA))

    for t in range(n_tiles_a):
        par = t % 2
        if t + 1 < n_tiles_a:
            tt = t + 1
            waits = [(d_x, 16 * (tt - 1))]
            if tt >= 2:
                waits.append((d_x2w, 16 * (tt - 1)))
            b.op("sync", lambda e, tt=tt: e.dma_start(
                x_sb[:, tt % 2], x_in[tt * TA:(tt + 1) * TA, :].rearrange("(s p) c -> p s c", p=128)),
                waits, d_x, 16)

        h1t_ready = 0
        for g in range(0, len(blocks), 4):
            grp = blocks[g:g + 4]
            s0, j0 = grp[0]
            ri = rot_next()
            hw_ = h1_pend_s[s0] if t == 0 else h1_pend
            waits = [(ss, hw_), (d_ms, 32), rot_state[ri]["free"]]
            for gi, (s, j) in enumerate(grp):
                m = b.op("tensor", lambda e, ri=ri, gi=gi, s=s, j=j, par=par: e.matmul(
                    rot[ri][:, gi * 128:(gi + 1) * 128], h1_sb[:, s, j * 128:(j + 1) * 128],
                    ident_sb[:, :], start=True, stop=True),
                    waits if gi == 0 else (), sp if gi == 3 else None)
            h1t_ready = b.op("scalar", lambda e, ri=ri, s0=s0, j0=j0: e.copy(
                h1t_sb[:, j0:j0 + 4, s0 * 128:(s0 + 1) * 128],
                rot[ri][:, :].rearrange("p (g c) -> p g c", c=128)), [(sp, m)], ss)
            rot_state[ri]["free"] = (ss, h1t_ready)
        h1_read_m[0] = sp.n

        q_done = 0
        for mi in range(8):
            ri = rot_next()
            waits = [(ss, h1t_ready), (d_wa, 16), rot_state[ri]["free"]]
            for k in range(8):
                m = b.op("tensor", lambda e, ri=ri, mi=mi, k=k: e.matmul(
                    rot[ri][:, 0:512], wq_sb[:, k, mi * 128:(mi + 1) * 128],
                    h1t_sb[:, k, :], start=(k == 0), stop=(k == 7)),
                    waits if k == 0 else (), sp if k == 7 else None)
            q_done = b.op("scalar", lambda e, ri=ri, mi=mi: e.copy(
                q_sb[:, mi, :], rot[ri][:, 0:512]), [(sp, m)], ss)
            rot_state[ri]["free"] = (ss, q_done)

        kv_k = 0
        kv_v = 0
        first_kev = True
        lastv = 0
        prevq = 0
        redq = [0, 0]
        for nn in (0, 1):
            for s in range(NSA):
                ri = rot_next()
                waits = [(ss, h1t_ready), (d_wa, 32), rot_state[ri]["free"]]
                for k in range(8):
                    m = b.op("tensor", lambda e, ri=ri, s=s, nn=nn, k=k: e.matmul(
                        rot[ri][:, 0:512], h1t_sb[:, k, s * 128:(s + 1) * 128],
                        wkv_sb[:, k, nn * 512:(nn + 1) * 512], start=(k == 0), stop=(k == 7)),
                        waits if k == 0 else (), sp if k == 7 else None)
                ew = [(sp, m)]
                if first_kev:
                    ew.append(attn_free[0])
                    first_kev = False
                kv_k = b.op("scalar", lambda e, ri=ri, s=s, nn=nn: e.copy(
                    k_sb[:, s, nn * 512:(nn + 1) * 512], rot[ri][:, 0:512]), ew, ss)
                rot_state[ri]["free"] = (ss, kv_k)
                if nn == 1:
                    lastv = b.op("vector", lambda e, s=s: e.reduce_sum(
                        sumx[:, s * 16:(s + 1) * 16],
                        k_sb[:, s].rearrange("p (h e) -> p h e", e=64), axis=AX.X),
                        [(ss, kv_k)] + ([sum_guard.get(0, (sv, 0))] if s == 0 else []), sv)
                    buf_k = ksq_sb if s % 2 == 0 else ksq2_sb
                    sx = b.op("scalar", lambda e, s=s, buf_k=buf_k: e.activation(
                        buf_k[:, :], k_sb[:, s], AF.Square),
                        [(sv, redq[s % 2])], ss)
                    prevq = b.op("vector", lambda e, s=s, buf_k=buf_k: e.reduce_sum(
                        sumq[:, s * 16:(s + 1) * 16],
                        buf_k[:, :].rearrange("p (h e) -> p h e", e=64), axis=AX.X),
                        [(ss, sx)], sv)
                    redq[s % 2] = prevq
        for nn in (2, 3):
            for s in range(NSA):
                ri = rot_next()
                waits = [(ss, h1t_ready), (d_wa, 48), rot_state[ri]["free"]]
                for k in range(8):
                    m = b.op("tensor", lambda e, ri=ri, s=s, nn=nn, k=k: e.matmul(
                        rot[ri][:, 0:512], h1t_sb[:, k, s * 128:(s + 1) * 128],
                        wkv_sb[:, k, nn * 512:(nn + 1) * 512], start=(k == 0), stop=(k == 7)),
                        waits if k == 0 else (), sp if k == 7 else None)
                kv_v = b.op("scalar", lambda e, ri=ri, s=s, nn=nn: e.copy(
                    v_sb[:, s, (nn - 2) * 512:(nn - 1) * 512], rot[ri][:, 0:512]), [(sp, m)], ss)
                rot_state[ri]["free"] = (ss, kv_v)
        if t == n_tiles_a - 1:
            gate_clsA_sp[0] = sp.n
            gate_clsA_ss[0] = ss.n
            gate_clsA_sv[0] = sv.n
            emit_wg_dmas()
            for bt_ in range(2):
                b.op("sync", lambda e, bt=bt_: e.dma_start(
                    wtens["x2"][:, bt], x2scr[bt * TB:(bt + 1) * TB, :].rearrange(
                        "(s p) c -> p s c", p=128)),
                    [(d_x2w, 16), (sp, gate_clsA_sp[0]), (ss, gate_clsA_ss[0]),
                     (sv, gate_clsA_sv[0]), (d_x2r, 16 * bt_)], d_x2r, 16)
        vch = ln_chain(slice(0, 64), DK, prevq, guard=stat_guard.get(0))
        sum_guard[0] = (sv, vch)
        knv_s = []
        knv = vch
        first_kn = [(sp, last_upd[0])]
        for s in range(NSA):
            a_b = bass.AP(aa, s * 16, [[96, 128], [1, 16], [0, 64]])
            b_b = bass.AP(bb, s * 16, [[96, 128], [1, 16], [0, 64]])
            knv = b.op("vector", lambda e, s=s, a_b=a_b: e.tensor_tensor(
                trash_v[:, :], k_sb[:, s], a_b, op=AL.mult),
                [(sv, knv), (sv, trash_tok[0])] + (first_kn if s == 0 else []), sv)
            knv = b.op("vector", lambda e, s=s, b_b=b_b: e.tensor_tensor(
                kn_sb[:, s], trash_v[:, :], b_b, op=AL.add), [(sv, knv)], sv)
            trash_tok[0] = knv
            knv_s.append(knv)
        stat_guard[0] = (sv, knv)

        kt_ev_s = []
        for s in range(NSA):
            ev = 0
            for half in range(2):
                j0 = 4 * half
                ri = rot_next()
                waits = [(sv, knv_s[s]), rot_state[ri]["free"]]
                for gi in range(4):
                    j = j0 + gi
                    m = b.op("tensor", lambda e, ri=ri, gi=gi, s=s, j=j: e.matmul(
                        rot[ri][:, gi * 128:(gi + 1) * 128], kn_sb[:, s, j * 128:(j + 1) * 128],
                        ident_sb[:, :], start=True, stop=True),
                        waits if gi == 0 else (), sp if gi == 3 else None)
                ev = b.op("scalar", lambda e, ri=ri, s=s, j0=j0: e.copy(
                    kt_sb[:, j0:j0 + 4, s * 128:(s + 1) * 128],
                    rot[ri][:, :].rearrange("p (g c) -> p g c", c=128)), [(sp, m)], ss)
                rot_state[ri]["free"] = (ss, ev)
            kt_ev_s.append(ev)

        pieces = a_ln_pieces(t + 1) if t + 1 < n_tiles_a else None

        def emit_scores(c):
            gcs = gscore[0]
            gscore[0] += 1
            cb = (gcs % 2) * 128
            war = (sp, upd_m_g.get(gcs - 2, 0))
            hw_masks = []
            for hw in range(2):
                sE = scE_ps if hw == 0 else rot[0]
                sO = scO_ps if hw == 0 else rot[1]
                first = True
                for jj in range(4 * hw, 4 * hw + 4):
                    he, ho = 2 * jj, 2 * jj + 1
                    col = (jj - 4 * hw) * 128
                    waits = [(ss, q_done), (ss, kt_ev_s[c]), (sv, last_sc_evac[hw])]
                    if hw == 1 and c == 0:
                        waits += [rot_state[0]["free"], rot_state[1]["free"]]
                    b.op("tensor", lambda e, he=he, col=col, c=c, sE=sE: e.matmul(
                        sE[:, col:col + 128], kt_sb[0:64, he // 2, c * 128:(c + 1) * 128],
                        q_sb[0:64, he // 2, c * 128:(c + 1) * 128],
                        start=True, stop=True, tile_position=(0, 0)),
                        waits if first else (), None)
                    first = False
                    m = b.op("tensor", lambda e, ho=ho, col=col, c=c, sO=sO: e.matmul(
                        sO[:, col:col + 128], kt_sb[64:128, ho // 2, c * 128:(c + 1) * 128],
                        q_sb[64:128, ho // 2, c * 128:(c + 1) * 128],
                        start=True, stop=True, tile_position=(64, 0)),
                        (), sp if jj == 4 * hw + 3 else None)
                mask_b = bass.AP(mask_sb, 0, [[128, 128], [0, 4], [1, 128]])
                ve = b.op("vector", lambda e, hw=hw, mask_b=mask_b, sE=sE, cb=cb: e.tensor_tensor(
                    sc_sb[:, 8 * hw:8 * hw + 8:2, cb:cb + 128],
                    sE[:, :].rearrange("p (j t) -> p j t", t=128), mask_b, op=AL.mult),
                    [(sp, m), war], sv)
                vo = b.op("vector", lambda e, hw=hw, mask_b=mask_b, sO=sO, cb=cb: e.tensor_tensor(
                    sc_sb[:, 8 * hw + 1:8 * hw + 8:2, cb:cb + 128],
                    sO[:, :].rearrange("p (j t) -> p j t", t=128), mask_b, op=AL.mult),
                    [(sv, ve)], sv)
                last_sc_evac[hw] = vo
                hw_masks.append(vo)
            return hw_masks, cb

        def emit_consumers(c, hw_masks, cb):
            gc = gchunk[0]
            for jj in range(8):
                he, ho = 2 * jj, 2 * jj + 1
                if jj == 0:
                    w0 = [(sv, hw_masks[0]), (ss, last_y_evac[0]), (ss, kv_v)]
                elif jj == 4:
                    w0 = [(sv, hw_masks[1])]
                else:
                    w0 = []
                stp = (gc == 0) and (jj in (3, 7))
                yp = y_psA if jj < 4 else y_psB
                col = (jj % 4) * 128
                b.op("tensor", lambda e, he=he, yp=yp, col=col, jj=jj, c=c, stp=stp, cb=cb: e.matmul(
                    yp[0:64, col:col + 128], v_sb[:, c, he * 64:(he + 1) * 64],
                    sc_sb[:, he, cb:cb + 128], start=(jj in (0, 4)), stop=stp,
                    tile_position=(0, 0)), w0, None)
                b.op("tensor", lambda e, ho=ho, yp=yp, col=col, jj=jj, c=c, stp=stp, cb=cb: e.matmul(
                    yp[64:128, col:col + 128], v_sb[:, c, ho * 64:(ho + 1) * 64],
                    sc_sb[:, ho, cb:cb + 128], start=(jj in (0, 4)), stop=stp,
                    tile_position=(0, 64), skip_group_check=True), (), None)
            if gc > 0:
                for jj in range(8):
                    he, ho = 2 * jj, 2 * jj + 1
                    w0 = [last_stcopy[0]] if jj == 0 else ()
                    stp = jj in (3, 7)
                    yp = y_psA if jj < 4 else y_psB
                    col = (jj % 4) * 128
                    b.op("tensor", lambda e, jj=jj, he=he, yp=yp, col=col, c=c, stp=stp: e.matmul(
                        yp[0:64, col:col + 128], st_sb[0:64, jj, :],
                        q_sb[0:64, he // 2, c * 128:(c + 1) * 128], start=False, stop=stp,
                        tile_position=(0, 0)), w0, None)
                    b.op("tensor", lambda e, jj=jj, ho=ho, yp=yp, col=col, c=c, stp=stp: e.matmul(
                        yp[64:128, col:col + 128], st_sb[64:128, jj, :],
                        q_sb[64:128, ho // 2, c * 128:(c + 1) * 128], start=False, stop=stp,
                        tile_position=(64, 64), skip_group_check=True), (), None)
            for jj in range(8):
                he, ho = 2 * jj, 2 * jj + 1
                w0 = [(sv, last_stcopy_v[0]), (ss, kv_v), (sv, knv)] if jj == 0 else ()
                b.op("tensor", lambda e, he=he, jj=jj, c=c, st=(jj == 0), fl=(jj == 7): e.matmul(
                    delta_ps[0:64, jj * 64:(jj + 1) * 64], kn_sb[:, c, he * 64:(he + 1) * 64],
                    v_sb[:, c, he * 64:(he + 1) * 64], start=st, stop=fl,
                    tile_position=(0, 0)), w0, None)
                m = b.op("tensor", lambda e, ho=ho, jj=jj, c=c, st=(jj == 0), fl=(jj == 7): e.matmul(
                    delta_ps[64:128, jj * 64:(jj + 1) * 64], kn_sb[:, c, ho * 64:(ho + 1) * 64],
                    v_sb[:, c, ho * 64:(ho + 1) * 64], start=st, stop=fl,
                    tile_position=(0, 64), skip_group_check=True), (), sp if jj == 7 else None)
            last_upd[0] = m
            last_inter[0] = m
            upd_m_g[gc] = m
            fold = b.op("vector", lambda e: e.scalar_tensor_tensor(
                stateF[:, :], delta_ps[:, :], 1.0, stateF[:, :],
                op0=AL.mult, op1=AL.add), [(sp, m)], sv)
            last_stcopy_v[0] = fold
            if t == n_tiles_a - 1:
                stc = b.op("vector", lambda e: e.tensor_copy(
                    st_sb[:, :, :], stateF[:, :].rearrange("p (j e) -> p j e", e=64)),
                    [(sv, fold)], sv)
                last_stcopy[0] = (sv, stc)
            else:
                stc = b.op("gpsimd", lambda e: e.tensor_copy(
                    st_sb[:, :, :], stateF[:, :].rearrange("p (j e) -> p j e", e=64)),
                    [(sv, fold)], sg)
                last_stcopy[0] = (sg, stc)
            b.op("scalar", lambda e, c=c: e.copy(
                y_sb[:, 0:4, c * 128:(c + 1) * 128],
                y_psA[:, :].rearrange("p (j t) -> p j t", t=128)), [(sp, m)], ss)
            last_y_evac[0] = b.op("scalar", lambda e, c=c: e.copy(
                y_sb[:, 4:8, c * 128:(c + 1) * 128],
                y_psB[:, :].rearrange("p (j t) -> p j t", t=128)), (), ss)
            y_ev_c[c] = last_y_evac[0]
            gchunk[0] += 1

        sc_cur = emit_scores(0)
        for c in range(NCH):
            if pieces is not None and c in (1, 2):
                (pieces[c - 1])()
            hw_masks, cb = sc_cur
            if c + 1 < NCH:
                sc_cur = emit_scores(c + 1)
            emit_consumers(c, hw_masks, cb)

        rot_state[0]["free"] = (sv, last_sc_evac[1])
        rot_state[1]["free"] = (sv, last_sc_evac[1])

        if pieces is not None:
            (pieces[2])()
            (pieces[3])()
        if t == n_tiles_a - 1:
            gate_clsB_sp[0] = sp.n
            gate_clsB_sv[0] = sv.n
            gate_clsB_ss[0] = ss.n

        base = 72 + 4 * par
        attn_ev = 0
        lastv = 0
        lasts = 0
        sg0 = [sum_guard.get(base, (sv, 0))]
        for s in range(NSA):
            for nn in range(2):
                ri = rot_next()
                waits = [(ss, y_ev_c[s]), (d_wa, 64), rot_state[ri]["free"]]
                for j in range(8):
                    m = b.op("tensor", lambda e, ri=ri, s=s, nn=nn, j=j: e.matmul(
                        rot[ri][:, 0:512], y_sb[:, j, s * 128:(s + 1) * 128],
                        wo_sb[:, j, nn * 512:(nn + 1) * 512], start=(j == 0), stop=(j == 7)),
                        waits if j == 0 else (), sp if j == 7 else None)
                attn_ev = b.op("scalar", lambda e, ri=ri, s=s, nn=nn: e.copy(
                    k_sb[:, s, nn * 512:(nn + 1) * 512], rot[ri][:, 0:512]),
                    [(sp, m), (sv, knv)], ss)
                rot_state[ri]["free"] = (ss, attn_ev)
                g2 = 2 * s + nn
                if pieces is not None and g2 <= 3:
                    h1_pend = pieces[4](g2)
            lastv = b.op("vector", lambda e, s=s, base=base: e.tensor_scalar(
                trash_v[:, :], k_sb[:, s], 1.0, 0.0, op0=AL.mult, op1=AL.add,
                accum_out=sumx[:, base + s:base + s + 1]),
                [(ss, attn_ev), (sv, trash_tok[0])], sv)
            trash_tok[0] = lastv
            lasts = b.op("scalar", lambda e, s=s, base=base: e.activation(
                sq_sb[:, :], k_sb[:, s], AF.Square,
                accum_out=sumq[:, base + s:base + s + 1]),
                [(ss, attn_ev), (ss, sq_tok[0])] + (sg0 if s == 0 else []), ss)
            sq_tok[0] = lasts

        lnb0 = None
        if t == n_tiles_a - 1:
            gate_clsC_sp[0] = sp.n
            gate_clsC_ss[0] = ss.n
            gate_clsC_sv[0] = sv.n
            emit_wu_dmas()
            if n_tiles_b > 0:
                lnb0 = b_ln_pieces(0)
                lnb0[0]()
        b.op("vector", lambda e: None, [(ss, lasts)])
        vch = ln_chain(slice(base, base + NSA), D, lastv, guard=stat_guard.get(base))
        sum_guard[base] = (sv, vch)
        if lnb0 is not None:
            lnb0[1]()
            h2_pend0[0] = lnb0[2]()
        nrm = vch
        residg = 0
        rsem = sv
        reng = "vector"
        for s in range(NSA):
            nrm = b.op("vector", lambda e, s=s, base=base: e.tensor_scalar(
                k_sb[:, s], k_sb[:, s], aa[:, base + s:base + s + 1],
                bb[:, base + s:base + s + 1], op0=AL.mult, op1=AL.add), [(sv, nrm)], sv)
            residg = b.op(reng, lambda e, s=s, par=par: e.tensor_tensor(
                x_sb[:, par, s], x_sb[:, par, s], k_sb[:, s], op=AL.add), [(sv, nrm)], rsem)
        stat_guard[base] = (rsem, residg)
        attn_free[0] = (rsem, residg)
        b.op("sync", lambda e, t=t, par=par: e.dma_start(
            x2scr[t * TA:(t + 1) * TA, :].rearrange("(s p) c -> p s c", p=128),
            x_sb[:, par]), [(rsem, residg), (d_x2w, 16 * t)], d_x2w, 16)

    spA, svA, ssA, sgA = sp.n, sv.n, ss.n, sg.n

    pctxA.close()
    actxA.close()
    bctx = ExitStack()

    def sbb(name, shape, dt):
        return bctx.enter_context(nc.sbuf_tensor(name, shape, dt))

    wg_sb = sbb("wg_sb", [128, 8, HID], BF16)
    wtens["wg"] = wg_sb
    x2_sb = sbb("x2_sb", [128, 2, NSB, 1024], F32)
    wtens["x2"] = x2_sb
    h2_sb = sbb("h2_sb", [128, 2, NSB, 1024], BF16)
    wtens["h2"] = h2_sb
    h2t_sb = sbb("h2t_sb", [128, 2, 8, TB], BF16)
    wu_sb = sbb("wu_sb", [128, 8, HID], BF16)
    wtens["wu"] = wu_sb
    sg_sb = sbb("sg_sb", [128, 24, TB], BF16)
    f_sb = sbb("f_sb", [128, NSB, 1024], F32)
    wd_sb = sbb("wd_sb", [128, 24, 1024], BF16)

    pb = [bctx.enter_context(nc.psum_tensor(f"pb{i}", [128, 512], F32)) for i in range(8)]

    a_done = [(sp, spA), (sv, svA), (ss, ssA), (sg, sgA), (d_x2w, 16 * n_tiles_a)]
    b.op("gpsimd", lambda e: e.dma_start(
        wd_sb[:, :, 0:512], wdt[:, 0:512].rearrange("(j p) c -> p j c", p=128)),
        a_done + [(d_wb, 16 * wdma_n[0])], d_wb, 16)
    wd_rdy = 16 * (wdma_n[0] + 1)
    wdma_n[0] += 1
    b.op("gpsimd", lambda e: e.dma_start(
        wd_sb[:, :, 512:1024], wdt[:, 512:1024].rearrange("(j p) c -> p j c", p=128)),
        [(d_wb, 16 * wdma_n[0])], d_wb, 16)
    wd_rdy2 = 16 * (wdma_n[0] + 1)
    wdma_n[0] += 1

    stt_tok = {}
    bank_state = [{"free": (sv, 0), "free_s": (ss, 0)} for _ in range(4)] + \
        [{"free": (sv, gate_clsC_sv[0]), "free_s": (ss, gate_clsC_ss[0])} for _ in range(4)]
    blocksB = [(s, j) for s in range(NSB) for j in range(8)]

    def b_transpose(tt, h2_last):
        par = tt % 2
        ht_ev = 0
        for g in range(0, 16, 4):
            grp = blocksB[g:g + 4]
            ri = (g // 4) % 4
            waits = [(sv, h2_last), (d_ms, 32), bank_state[ri]["free"], bank_state[ri]["free_s"]]
            if tt == 0 and g == 0:
                waits.append((sp, gate_clsC_sp[0]))
                waits.append((ss, gate_clsC_ss[0]))
            for gi, (s, j) in enumerate(grp):
                m = b.op("tensor", lambda e, ri=ri, gi=gi, s=s, j=j, par=par: e.matmul(
                    pb[ri][:, gi * 128:(gi + 1) * 128], h2_sb[:, par, s, j * 128:(j + 1) * 128],
                    ident_sb[:, :], start=True, stop=True),
                    waits if gi == 0 else (), sp if gi == 3 else None)
            s0, j0 = grp[0]
            ev = b.op("vector", lambda e, ri=ri, s0=s0, j0=j0, par=par: e.tensor_copy(
                h2t_sb[:, par, j0:j0 + 4, s0 * 128:(s0 + 1) * 128],
                pb[ri][:, :].rearrange("p (g c) -> p g c", c=128)), [(sp, m)], sv)
            bank_state[ri]["free"] = (sv, ev)
            ht_ev = ev
        trans_m[par] = sp.n
        return ht_ev

    def make_fln(t, f_ev_s):
        par = t % 2
        base = 88 + 4 * par
        st = {}
        last = (t == n_tiles_b - 1)
        reng_b, rsem_b = ("vector", sv) if last else ("gpsimd", sg)

        if last:
            def p_last():
                sg0 = [sum_guard.get(base, (sv, 0))]
                lv = 0
                lq = 0
                for s in range(NSB):
                    lv = b.op("vector", lambda e, s=s: e.tensor_scalar(
                        trash_v[:, :], f_sb[:, s], 1.0, 0.0, op0=AL.mult, op1=AL.add,
                        accum_out=sumx[:, base + s:base + s + 1]),
                        [(ss, f_ev_s[s]), (sv, trash_tok[0])], sv)
                    trash_tok[0] = lv
                    lq = b.op("scalar", lambda e, s=s: e.activation(
                        sq_sb[:, :], f_sb[:, s], AF.Square,
                        accum_out=sumq[:, base + s:base + s + 1]),
                        [(ss, f_ev_s[s]), (ss, sq_tok[0])] + (sg0 if s == 0 else []), ss)
                    sq_tok[0] = lq
                b.op("vector", lambda e: None, [(ss, lq)])
                vch = ln_chain(slice(base, base + NSB), D, lv)
                for s in range(NSB):
                    nrm = b.op("vector", lambda e, s=s: e.tensor_scalar(
                        f_sb[:, s], f_sb[:, s], aa[:, base + s:base + s + 1],
                        bb[:, base + s:base + s + 1], op0=AL.mult, op1=AL.add),
                        [(sv, vch)], sv)
                    residg = b.op("vector", lambda e, s=s: e.tensor_tensor(
                        f_sb[:, s], x2_sb[:, par, s], f_sb[:, s], op=AL.add),
                        [(sv, nrm)], sv)
                    b.op("sync", lambda e, s=s: e.dma_start(
                        out_d[t * TB + s * 128:t * TB + (s + 1) * 128, :].rearrange(
                            "(s p) c -> p s c", p=128), f_sb[:, s:s + 1]),
                        [(sv, residg), (d_out, 16 * (t + s))], d_out, 16)

            def p_nop():
                pass
            return [p_nop, p_nop, p_last]

        def p0():
            lastv = 0
            for s in range(NSB):
                lastv = b.op("vector", lambda e, s=s: e.tensor_scalar(
                    trash_v[:, :], f_sb[:, s], 1.0, 0.0, op0=AL.mult, op1=AL.add,
                    accum_out=sumx[:, base + s:base + s + 1]),
                    [(ss, f_ev_s[s]), (sv, trash_tok[0])], sv)
                trash_tok[0] = lastv
            lasts = 0
            sg0 = [sum_guard.get(base, (sv, 0))]
            for s in range(NSB):
                lasts = b.op("scalar", lambda e, s=s: e.activation(
                    sq_sb[:, :], f_sb[:, s], AF.Square,
                    accum_out=sumq[:, base + s:base + s + 1]),
                    [(ss, f_ev_s[s]), (ss, sq_tok[0])] + (sg0 if s == 0 else []), ss)
                sq_tok[0] = lasts
            st["lastv"], st["lasts"] = lastv, lasts

        def p1():
            b.op("vector", lambda e: None, [(ss, st["lasts"])])
            vch = ln_chain(slice(base, base + NSB), D, st["lastv"])
            sum_guard[base] = (sv, vch)
            st["vch"] = vch

        def p2():
            nrm = st["vch"]
            residg = 0
            for s in range(NSB):
                nrm = b.op("vector", lambda e, s=s: e.tensor_scalar(
                    f_sb[:, s], f_sb[:, s], aa[:, base + s:base + s + 1],
                    bb[:, base + s:base + s + 1], op0=AL.mult, op1=AL.add), [(sv, nrm)], sv)
                residg = b.op(reng_b, lambda e, s=s: e.tensor_tensor(
                    f_sb[:, s], x2_sb[:, par, s], f_sb[:, s], op=AL.add), [(sv, nrm)], rsem_b)
            resid_done[0] = residg
            b.op("sync", lambda e: e.dma_start(
                out_d[t * TB:(t + 1) * TB, :].rearrange("(s p) c -> p s c", p=128),
                f_sb[:, :]), [(rsem_b, residg), (d_out, 16 * t)], d_out, 16)
        return [p0, p1, p2]

    if n_tiles_b > 0:
        ht_pend = b_transpose(0, h2_pend0[0])

    fln_pend = [None]
    for t in range(n_tiles_b):
        par = t % 2
        lnb = b_ln_pieces(t + 1) if t + 1 < n_tiles_b else None
        fp = fln_pend[0]

        h_ready = 0
        if t == 0:
            e1_tok = {}
            for mi in range(24):
                rg = mi % 8
                gw = [(sv, ht_pend), (d_wb, wg_rdy[mi // 6]),
                      bank_state[rg]["free"], bank_state[rg]["free_s"]]
                for k in range(8):
                    mg = b.op("tensor", lambda e, rg=rg, mi=mi, k=k, par=par: e.matmul(
                        pb[rg][:, 0:TB], wg_sb[:, k, mi * 128:(mi + 1) * 128],
                        h2t_sb[:, par, k, :], start=(k == 0), stop=(k == 7)),
                        gw if k == 0 else (), sp if k == 7 else None)
                e1 = b.op("scalar", lambda e, rg=rg, mi=mi: e.activation(
                    sg_sb[:, mi, :], pb[rg][:, 0:TB], AF.Silu, scale=SC_FF),
                    [(sp, mg)] + (a_done if mi == 0 else []), ss)
                bank_state[rg]["free_s"] = (ss, e1)
                e1_tok[mi] = e1
                if lnb is not None:
                    if mi == 13:
                        lnb[0]()
                    elif mi == 15:
                        lnb[1]()
                    elif mi == 17:
                        h2_next = lnb[2]()
            for mi in range(24):
                ru = mi % 8
                uw = [(d_wb, wu_rdy[mi // 12]),
                      bank_state[ru]["free"], bank_state[ru]["free_s"]]
                for k in range(8):
                    mu = b.op("tensor", lambda e, ru=ru, mi=mi, k=k, par=par: e.matmul(
                        pb[ru][:, 0:TB], wu_sb[:, k, mi * 128:(mi + 1) * 128],
                        h2t_sb[:, par, k, :], start=(k == 0), stop=(k == 7)),
                        uw if k == 0 else (), sp if k == 7 else None)
                e2 = b.op("vector", lambda e, ru=ru, mi=mi: e.scalar_tensor_tensor(
                    sg_sb[:, mi, :], pb[ru][:, 0:TB], 1.0, sg_sb[:, mi, :],
                    op0=AL.mult, op1=AL.mult), [(sp, mu), (ss, e1_tok[mi])], sv)
                bank_state[ru]["free"] = (sv, e2)
                stt_tok[mi] = e2
                h_ready = e2
        else:
            for mi in range(24):
                rg = (2 * mi) % 8
                ru = (2 * mi + 1) % 8
                gw = [(sv, ht_pend), (d_wb, wg_rdy[mi // 6]), bank_state[rg]["free"], bank_state[rg]["free_s"]]
                for k in range(8):
                    mg = b.op("tensor", lambda e, rg=rg, mi=mi, k=k, par=par: e.matmul(
                        pb[rg][:, 0:TB], wg_sb[:, k, mi * 128:(mi + 1) * 128],
                        h2t_sb[:, par, k, :], start=(k == 0), stop=(k == 7)),
                        gw if k == 0 else (), sp if k == 7 else None)
                uw = [(d_wb, wu_rdy[mi // 12]), bank_state[ru]["free"], bank_state[ru]["free_s"]]
                for k in range(8):
                    mu = b.op("tensor", lambda e, ru=ru, mi=mi, k=k, par=par: e.matmul(
                        pb[ru][:, 0:TB], wu_sb[:, k, mi * 128:(mi + 1) * 128],
                        h2t_sb[:, par, k, :], start=(k == 0), stop=(k == 7)),
                        uw if k == 0 else (), sp if k == 7 else None)
                e1 = b.op("scalar", lambda e, rg=rg, mi=mi: e.activation(
                    sg_sb[:, mi, :], pb[rg][:, 0:TB], AF.Silu, scale=SC_FF),
                    [(sp, mg)], ss)
                bank_state[rg]["free_s"] = (ss, e1)
                e2 = b.op("vector", lambda e, ru=ru, mi=mi: e.scalar_tensor_tensor(
                    sg_sb[:, mi, :], pb[ru][:, 0:TB], 1.0, sg_sb[:, mi, :],
                    op0=AL.mult, op1=AL.mult), [(sp, mu), (ss, e1)], sv)
                bank_state[ru]["free"] = (sv, e2)
                stt_tok[mi] = e2
                h_ready = e2
                if fp is not None:
                    if mi == 1:
                        fp[0]()
                    elif mi == 3:
                        fp[1]()
                    elif mi == 5:
                        fp[2]()
                if lnb is not None:
                    if mi == 13:
                        lnb[0]()
                    elif mi == 15:
                        lnb[1]()
                    elif mi == 17:
                        h2_next = lnb[2]()

        if lnb is not None:
            ht_pend = b_transpose(t + 1, h2_next)

        f_ev = 0
        f_ev_s = [0] * NSB
        for s in range(NSB):
            for nn in range(2):
                ri = 4 + 2 * s + nn
                wdw = (d_wb, wd_rdy) if nn == 0 else (d_wb, wd_rdy2)
                waits = [wdw, bank_state[ri]["free"], bank_state[ri]["free_s"]]
                for kh in range(24):
                    m = b.op("tensor", lambda e, ri=ri, s=s, nn=nn, kh=kh: e.matmul(
                        pb[ri][:, 0:512], sg_sb[:, kh, s * 128:(s + 1) * 128],
                        wd_sb[:, kh, nn * 512:(nn + 1) * 512], start=(kh == 0), stop=(kh == 23)),
                        (waits if kh == 0 else []) + [(sv, stt_tok[kh])], sp if kh == 23 else None)
                f_ev = b.op("scalar", lambda e, ri=ri, s=s, nn=nn: e.activation(
                    f_sb[:, s, nn * 512:(nn + 1) * 512], pb[ri][:, 0:512],
                    AF.Identity, scale=SC_DN), [(sp, m), (d_out, 16 * t)], ss)
                bank_state[ri]["free_s"] = (ss, f_ev)
            f_ev_s[s] = f_ev

        fln_pend[0] = make_fln(t, f_ev_s)

    if fln_pend[0] is not None:
        fp = fln_pend[0]
        fp[0]()
        fp[1]()
        fp[2]()

    finals = [(d_out, 16 * (n_tiles_b + 1)), (d_x2r, 16 * n_tiles_b), (d_x2w, 16 * n_tiles_a),
              (d_x, 16 * (n_tiles_a - 1)), (d_x0, 16 * NSA), (d_wa, 64), (d_wb, 128),
              (d_ms, 32)]
    b.op("sync", lambda e: None, [(s, v) for s, v in finals if v > 0])

    emit(nc, b, ctx)
    bctx.close()
    ctx.close()
    return nc


_mask_np = np.triu(np.ones((128, 128), dtype=np.float32)).astype(ml_dtypes.bfloat16)
_ident_np = np.eye(128, dtype=np.float32).astype(ml_dtypes.bfloat16)


def kernel(L0_attn_x, L0_attn_W_qkv, L0_attn_W_o, L0_ffn_w_gate, L0_ffn_w_up, L0_ffn_w_down):
    x = np.ascontiguousarray(np.asarray(L0_attn_x, dtype=np.float32))
    wqkv = np.ascontiguousarray(np.asarray(L0_attn_W_qkv, dtype=np.float32))
    wo = np.ascontiguousarray(np.asarray(L0_attn_W_o, dtype=np.float32))
    wgt = np.ascontiguousarray(np.asarray(L0_ffn_w_gate, dtype=np.int32).T)
    wut = np.ascontiguousarray(np.asarray(L0_ffn_w_up, dtype=np.int32).T)
    wdt = np.ascontiguousarray(np.asarray(L0_ffn_w_down, dtype=np.int32).T)

    nc = build()
    in_maps = []
    for bt in range(8):
        in_maps.append({
            "x_in": x[bt], "wqkv": wqkv, "wo": wo,
            "wgt": wgt, "wut": wut, "wdt": wdt,
            "maskd": _mask_np, "identd": _ident_np,
        })
    res = run_bass_kernel_spmd(nc, in_maps, core_ids=list(range(8)))
    return np.stack([res.results[i]["out"] for i in range(8)], axis=0)
